# version 03
# baseline (speedup 1.0000x reference)
import math
import os
import numpy as np
import ml_dtypes
from contextlib import ExitStack

import concourse.bass as bass
import concourse.mybir as mybir
from concourse.bass_utils import run_bass_kernel_spmd

F32 = mybir.dt.float32
BF16 = mybir.dt.bfloat16
AF = mybir.ActivationFunctionType
ALU = mybir.AluOpType
AX = mybir.AxisListType

N_CORES = 8
B = 16
C = 2048
D = 2048
NH = 16
HD = 128
HD2 = 64
NC = D // 128
NTT = C // 128
SEQ_PER_CORE = B // N_CORES
N_EXP = 32
F_EXP = 768
F_SH = 1536
CHUNK = 1024
EPS = 1e-5
S128 = 1.0 / math.sqrt(128.0)
MROT = 128.0 ** (-0.25)
FT_W = 256
NPQ = D // FT_W

ENGINES = ("tensor", "vector", "scalar", "gpsimd", "sync")
_SEM_ENGINE = {}


class Tok:
    __slots__ = ("sem", "val")
    def __init__(self, sem, val):
        self.sem = sem
        self.val = val


class Sched:
    def __init__(self, nc, es):
        self.nc = nc
        self.ops = {e: [] for e in ENGINES}
        self.cnt = {}
        self.sems = {}
        self.hwm = {e: {} for e in ENGINES}
        self.es = es
        self.free = {}
        self.ring = 0
        self.nops = 0
        self.limit = int(os.environ.get("K_MAXOPS", "0"))

    def sem(self, name):
        if name not in self.sems:
            self.sems[name] = self.es.enter_context(self.nc.semaphore(name))
            self.cnt[name] = 0
        return self.sems[name]

    def dep(self, key):
        return self.free.get(key)

    def setf(self, key, tok):
        self.free[key] = tok

    def emit(self, eng, fn, deps=(), inc=None, inc_by=1):
        self.nops += 1
        if self.limit and self.nops > self.limit:
            tok = None
            if inc is not None:
                self.sem(inc)
                self.cnt[inc] += inc_by
                tok = Tok(inc, self.cnt[inc])
                sem_h = self.sems[inc]
                self.ops[eng].append(
                    lambda e, s_=sem_h, k_=inc_by: e.sem_inc(s_, k_))
            return tok
        tok = None
        if inc is not None:
            self.sem(inc)
            self.cnt[inc] += inc_by
            tok = Tok(inc, self.cnt[inc])
        byname = {}
        own = _SEM_ENGINE.get(eng, ())
        for d in deps:
            if d is None:
                continue
            if d.sem in own:
                continue
            if byname.get(d.sem, -1) < d.val:
                byname[d.sem] = d.val
        waits = []
        hw = self.hwm[eng]
        for nm, val in byname.items():
            if hw.get(nm, -1) >= val:
                continue
            hw[nm] = val
            waits.append((self.sems[nm], val))
        inc_sem = self.sems[inc] if inc is not None else None

        def closure(e, waits=waits, fn=fn, inc_sem=inc_sem, inc_by=inc_by):
            for s, v in waits:
                e.wait_ge(s, v)
            ins = fn(e)
            if inc_sem is not None:
                ins.then_inc(inc_sem, inc_by)
        self.ops[eng].append(closure)
        return tok

    def wait_only(self, eng, toks):
        self.emit(eng, lambda e: None, deps=toks)

    def final_toks(self):
        return [Tok(s, self.cnt[s]) for s in self.sems if self.cnt[s] > 0]

    def replay(self, block):
        for eng in ENGINES:
            ops = self.ops[eng]
            if not ops:
                continue
            dec = getattr(block, eng)

            def section(e, ops=ops):
                for f in ops:
                    f(e)
            dec(section)
        self.ops = {e: [] for e in ENGINES}


class TSlots:
    def __init__(self, pst):
        self.pst = pst
        self.i = 0
    def get(self):
        b = self.i % 2
        s2 = (self.i // 2) % 8
        self.i += 1
        return (self.pst[:, b, s2 * 128:(s2 + 1) * 128], ("pst", b, s2), ("pstB", b))


class VecSlots:
    def __init__(self, t, n):
        self.t = t
        self.n = n
        self.i = 0
    def get(self):
        i = self.i
        self.i = (self.i + 1) % self.n
        return self.t[:, i:i + 1]


def _vec_pc(v):
    return np.ascontiguousarray(np.asarray(v).reshape(-1, 128).T)


def prep_host(inputs):
    ii = inputs
    sh = {}
    def pm(w, d_in, blk):
        w = np.asarray(w)
        f = w.shape[1]
        return np.ascontiguousarray(
            w.reshape(d_in // 128, 128, f // blk, blk).transpose(2, 1, 0, 3))
    sh["WqS"] = pm(ii["L0_attn_Wq"], D, FT_W)
    sh["WkvS"] = pm(ii["L0_attn_Wkv"], D, FT_W)
    sh["WoS"] = np.ascontiguousarray(
        np.asarray(ii["L0_attn_Wo"]).reshape(NH, 128, 4, 512).transpose(0, 2, 1, 3))
    sh["gwS"] = np.ascontiguousarray(
        np.asarray(ii["L0_ffn_gate_w"]).reshape(NC, 128, N_EXP).transpose(1, 0, 2))
    sh["WgS"] = np.ascontiguousarray(
        np.asarray(ii["L0_ffn_wg"]).reshape(N_EXP, NC, 128, F_EXP).transpose(0, 2, 1, 3))
    sh["WuS"] = np.ascontiguousarray(
        np.asarray(ii["L0_ffn_wu"]).reshape(N_EXP, NC, 128, F_EXP).transpose(0, 2, 1, 3))
    sh["WdS"] = np.ascontiguousarray(
        np.asarray(ii["L0_ffn_wd"]).reshape(N_EXP, 6, 128, D).transpose(0, 2, 1, 3))
    sh["SgS"] = np.ascontiguousarray(
        np.asarray(ii["L0_ffn_sg"]).reshape(NC, 128, 2, 768).transpose(2, 1, 0, 3))
    sh["SuS"] = np.ascontiguousarray(
        np.asarray(ii["L0_ffn_su"]).reshape(NC, 128, 2, 768).transpose(2, 1, 0, 3))
    sh["SdS"] = np.ascontiguousarray(
        np.asarray(ii["L0_ffn_sd"]).reshape(12, 128, 2, 1024).transpose(2, 1, 0, 3))
    sh["gq"] = _vec_pc(ii["L0_attn_ln_q_g"]); sh["bqpre"] = _vec_pc(ii["L0_attn_ln_q_b"])
    sh["gkv"] = _vec_pc(ii["L0_attn_ln_kv_g"]); sh["bkvpre"] = _vec_pc(ii["L0_attn_ln_kv_b"])
    sh["bq_row"] = np.ascontiguousarray(np.asarray(ii["L0_attn_bq"]).reshape(1, D))
    sh["bkv_row"] = np.ascontiguousarray(np.asarray(ii["L0_attn_bkv"]).reshape(1, 2 * D))
    sh["bo_row"] = np.ascontiguousarray(np.asarray(ii["L0_attn_bo"]).reshape(1, D))
    sh["lnh_g"] = np.ascontiguousarray(np.asarray(ii["L0_attn_ln_h_g"]).reshape(128, 1))
    sh["lnh_b"] = np.ascontiguousarray(np.asarray(ii["L0_attn_ln_h_b"]).reshape(128, 1))
    sh["c_eye"] = np.eye(128, dtype=np.float32).astype(ml_dtypes.bfloat16)
    sh["c_eye_f"] = np.eye(128, dtype=np.float32)
    sh["c_ones_row"] = np.ones((1, 128), dtype=ml_dtypes.bfloat16)
    sh["c_ones_col"] = np.ones((128, 1), dtype=ml_dtypes.bfloat16)
    f = (np.float32(8000.0) / np.float32(200.0)) ** np.linspace(
        0.0, 1.0, HD2, dtype=np.float32) * np.float32(200.0) / np.float32(1000.0)
    freqs = (np.float32(200.0) * f / np.float32(1000.0)).astype(np.float32)
    t = np.arange(C, dtype=np.float32)
    ang = np.einsum("i,j->ij", t, freqs).astype(np.float32)
    sh["c_cos"] = np.ascontiguousarray(
        np.cos(ang).astype(np.float32).reshape(NTT, 128, HD2).transpose(1, 0, 2))
    sh["c_sin"] = np.ascontiguousarray(
        np.sin(ang).astype(np.float32).reshape(NTT, 128, HD2).transpose(1, 0, 2))
    xs = [np.ascontiguousarray(np.asarray(inputs["L0_attn_x"])[2 * c:2 * c + 2])
          for c in range(N_CORES)]
    return sh, xs


def emit_ln_stats(sch, sb, vs, x_ap, W, deps, double_ln, mrot_slot=None):
    lvl = int(os.environ.get("K_STLEVEL", "99"))
    if lvl <= 0:
        t0 = sch.emit("vector", lambda e: e.memset(sb["vec"][:, 0:1], 1.0),
                      deps=list(deps), inc="v")
        return sb["cbias"][:, 0:1], sb["cbias"][:, 1:2], [t0]
    """LN stats over a [128, W] fp32 sbuf tile -> (rc, bc, toks): y = x*rc + bc.
    double_ln: combined LN(LN(x)) normalization (inner mean is exactly 0);
    optionally m' = sqrt(sum(LN(x)^2)) * MROT."""
    ssq = vs.get(); ssum = vs.get()
    if os.environ.get("K_NOSQ"):
        t_sq0 = sch.emit("vector", lambda e: e.tensor_mul(
            sb["dummy_f"][:, 0:W], x_ap, x_ap),
            deps=list(deps) + [sch.dep("dummy")], inc="v")
        t_sq = sch.emit("vector", lambda e: e.reduce_sum(
            ssq, sb["dummy_f"][:, 0:W], axis=AX.X), deps=[t_sq0], inc="v")
    else:
        t_sq = sch.emit("scalar", lambda e: e.activation(
            out=sb["dummy_bf"][:, 0:W], in_=x_ap, func=AF.Square, accum_out=ssq),
            deps=list(deps) + [sch.dep("dummy")], inc="s")
    t_sm = sch.emit("vector", lambda e: e.reduce_sum(ssum, x_ap, axis=AX.X),
                    deps=deps, inc="v")
    if lvl <= 1:
        return sb["cbias"][:, 0:1], sb["cbias"][:, 1:2], [t_sq, t_sm]
    inv = 1.0 / W
    mu = vs.get(); e2 = vs.get(); tmp = vs.get()
    ta = sch.emit("vector", lambda e, mu_=mu, ssum_=ssum: e.tensor_scalar(
        out=mu_, in0=ssum_, scalar1=inv, scalar2=None, op0=ALU.mult), deps=[t_sm], inc="v")
    tb = sch.emit("vector", lambda e, t_=tmp, ssum_=ssum: e.scalar_tensor_tensor(
        out=t_, in0=ssum_, scalar=inv, in1=ssum_, op0=ALU.mult, op1=ALU.mult),
        deps=[t_sm], inc="v")
    tc = sch.emit("vector", lambda e, e2_=e2, ssq_=ssq, t_=tmp: e.tensor_sub(e2_, ssq_, t_),
                  deps=[t_sq, tb], inc="v")
    if lvl <= 2:
        return sb["cbias"][:, 0:1], sb["cbias"][:, 1:2], [t_sq, t_sm, ta, tb, tc]
    lntmp = vs.get(); r1 = vs.get()
    ts1 = sch.emit("scalar", lambda e, a=lntmp, b=e2: e.activation(
        out=a, in_=b, func=AF.Ln, scale=inv, bias=sb["cbias"][:, 0:1]), deps=[tc], inc="s")
    ts2 = sch.emit("scalar", lambda e, a=r1, b=lntmp: e.activation(
        out=a, in_=b, func=AF.Exp, scale=-0.5), deps=[ts1], inc="s")
    toks = [t_sq, t_sm, ta, tb, tc, ts1, ts2]
    if lvl <= 3:
        return sb["cbias"][:, 0:1], sb["cbias"][:, 1:2], toks
    if not double_ln:
        bc = vs.get()
        tbc = sch.emit("vector", lambda e, a=bc, b=mu, c=r1: e.scalar_tensor_tensor(
            out=a, in0=b, scalar=-1.0, in1=c, op0=ALU.mult, op1=ALU.mult),
            deps=[ts2], inc="v")
        return r1, bc, toks + [tbc]
    r1sq = vs.get(); s2 = vs.get()
    td = sch.emit("vector", lambda e, a=r1sq, b=r1: e.scalar_tensor_tensor(
        out=a, in0=b, scalar=1.0, in1=b, op0=ALU.mult, op1=ALU.mult), deps=[ts2], inc="v")
    te = sch.emit("vector", lambda e, a=s2, b=e2, c=r1sq: e.tensor_mul(a, b, c),
                  deps=[td, tc], inc="v")
    lnt2 = vs.get(); r2 = vs.get(); rc = vs.get(); bc = vs.get()
    tf = sch.emit("scalar", lambda e, a=lnt2, b=s2: e.activation(
        out=a, in_=b, func=AF.Ln, scale=inv, bias=sb["cbias"][:, 0:1]), deps=[te], inc="s")
    tg = sch.emit("scalar", lambda e, a=r2, b=lnt2: e.activation(
        out=a, in_=b, func=AF.Exp, scale=-0.5), deps=[tf], inc="s")
    th = sch.emit("vector", lambda e, a=rc, b=r1, c=r2: e.tensor_mul(a, b, c),
                  deps=[tg], inc="v")
    ti = sch.emit("vector", lambda e, a=bc, b=mu, c=rc: e.scalar_tensor_tensor(
        out=a, in0=b, scalar=-1.0, in1=c, op0=ALU.mult, op1=ALU.mult), deps=[th], inc="v")
    toks += [td, te, tf, tg, th, ti]
    if mrot_slot is not None:
        lnm = vs.get()
        tj = sch.emit("scalar", lambda e, a=lnm, b=s2: e.activation(
            out=a, in_=b, func=AF.Ln), deps=[te], inc="s")
        tk = sch.emit("scalar", lambda e, a=mrot_slot, b=lnm: e.activation(
            out=a, in_=b, func=AF.Exp, scale=0.5, bias=sb["cbias"][:, 1:2]),
            deps=[tj], inc="s")
        toks += [tj, tk]
    return rc, bc, toks


def emit_attention(nc, sch, sb, ap, es, persist_tok, qn_scr, kn_scr, v_scr,
                   xmid_scr, n_seq, n_qg=4, n_ft=24, n_tt=NTT):
    vs = VecSlots(sb["vec"], 64)
    at = {}
    def sbt(name, shape, dt=F32):
        at[name] = es.enter_context(nc.sbuf_tensor("a_" + name, shape, dt))
        return at[name]
    sbt("xnfm", [128, NC, C], BF16)
    sbt("wsl", [128, 2, NC, FT_W], BF16)
    sbt("stage", [128, 2, 4096], BF16)
    sbt("attT", [128, 3, 512], BF16)
    sbt("afm", [128, NH, 512], BF16)
    sbt("kn", [128, C], BF16)
    sbt("vv", [128, NTT, 128], BF16)
    sbt("qnb", [128, 512], BF16)
    sbt("woring", [128, 3, 512], BF16)
    sbt("rre", [128, NTT, 128], BF16)
    sbt("rse", [128, NTT, 128], BF16)
    sbt("o_bf", [128, 4, D], BF16)
    sbt("xf", [128, 2, D], F32)
    sbt("of", [128, D], F32)
    sbt("xnt", [128, D], BF16)
    sbt("qrot", [128, FT_W], F32)
    sbt("qsw", [128, FT_W], F32)
    sbt("qn_tm", [128, 3, FT_W], BF16)
    sbt("brow", [1, 2, FT_W], BF16)
    sbt("rowbuf", [1, 2, FT_W])
    sbt("sumb", [1, 2, 512], BF16)
    sbt("recip", [128, 512], F32)
    sbt("hvec", [128, 16])
    sbt("cosT", [128, NTT, HD2]); sbt("sinT", [128, NTT, HD2])

    ps = es.enter_context(nc.psum_tensor("psA", [128, 6, 512], F32))
    pst = es.enter_context(nc.psum_tensor("psT", [128, 2, 1024], BF16))
    tsl = TSlots(pst)
    eye = sb["eye"]
    dep, setf = sch.dep, sch.setf
    if os.environ.get("K_NOSQ"):
        sb["dummy_f"] = at["of"]
    t_ct1 = sch.emit("sync", lambda e: e.dma_start(out=at["cosT"][:], in_=ap["c_cos"]),
                     inc="sct", inc_by=16)
    t_ct2 = sch.emit("sync", lambda e: e.dma_start(out=at["sinT"][:], in_=ap["c_sin"]),
                     inc="sct", inc_by=16)
    tab_tok = [Tok("sct", 32)]

    for s in range(n_seq):
        for tt in range(n_tt):
            xb = at["xf"][:, tt % 2, :]
            t_x = sch.emit("sync", lambda e, d=xb, s_=s, tt_=tt: e.dma_start(
                out=d, in_=ap["x"][s_, tt_ * 128:(tt_ + 1) * 128, :]),
                deps=[dep(("xf", tt % 2))] + (persist_tok if (s, tt) == (0, 0) else []),
                inc="sx%d" % (tt % 2), inc_by=16)
            mslot = sb["mrot"][:, tt:tt + 1]
            if os.environ.get("K_NOSTATS"):
                t_f1 = sch.emit("vector", lambda e, m=mslot: e.memset(m, 1.0), deps=[t_x], inc="v")
                rc, bc = sb["cbias"][:, 0:1], sb["cbias"][:, 1:2]
                stoks = [t_f1]
            elif os.environ.get("K_LN1"):
                rc, bc, stoks = emit_ln_stats(sch, sb, vs, xb, D, [t_x], False)
                t_f1 = sch.emit("vector", lambda e, m=mslot: e.memset(m, 1.0), deps=[t_x], inc="v")
                stoks = stoks + [t_f1]
            else:
                rc, bc, stoks = emit_ln_stats(sch, sb, vs, xb, D, [t_x], True, mslot)
            if os.environ.get("K_FAKEAFF"):
                rc, bc = sb["cbias"][:, 0:1], sb["cbias"][:, 1:2]
            mtok = stoks[-1]
            cos_t, sin_t = at["cosT"][:, tt, :], at["sinT"][:, tt, :]
            do_rre = not os.environ.get("K_NORRE")
            rre_t, rse_t = at["rre"][:, tt, :], at["rse"][:, tt, :]
            if do_rre:
                sch.emit("vector", lambda e, o=rre_t, i=cos_t, m=mslot: e.tensor_scalar(
                    out=o[:, 0:128:2], in0=i, scalar1=m, scalar2=None, op0=ALU.mult),
                    deps=[mtok] + tab_tok, inc=None)
                sch.emit("vector", lambda e, o=rre_t, i=cos_t, m=mslot: e.tensor_scalar(
                    out=o[:, 1:128:2], in0=i, scalar1=m, scalar2=None, op0=ALU.mult),
                    inc=None)
                sch.emit("vector", lambda e, o=rse_t, i=sin_t, m=mslot: e.tensor_scalar(
                    out=o[:, 0:128:2], in0=i, scalar1=m, scalar2=-1.0, op0=ALU.mult,
                    op1=ALU.mult), inc=None)
                t_rt = sch.emit("vector", lambda e, o=rse_t, i=sin_t, m=mslot: e.tensor_scalar(
                    out=o[:, 1:128:2], in0=i, scalar1=m, scalar2=None, op0=ALU.mult),
                    inc="v")
                setf(("rot", tt), t_rt)
            t_xn = sch.emit("vector", lambda e, xb_=xb, rc_=rc, bc_=bc: e.tensor_scalar(
                out=at["xnt"][:], in0=xb_, scalar1=rc_, scalar2=bc_,
                op0=ALU.mult, op1=ALU.add), deps=stoks + [dep("xnt")], inc="v")
            setf(("xf", tt % 2), t_xn)
            last_ev = None
            for c in range(NC if not os.environ.get("K_NOTP") else 0):
                pap, pkey, bkey = tsl.get()
                t_tr = sch.emit("tensor", lambda e, c_=c, p_=pap: e.transpose(
                    p_, at["xnt"][:, c_ * 128:(c_ + 1) * 128],
                    eye[:]), deps=[t_xn, dep(pkey), dep(bkey), dep("xnfm_war")], inc="tr")
                ev_eng = "vector" if os.environ.get("K_EVAC_V") else "scalar"
                t_ev = sch.emit("scalar", lambda e, c_=c, tt_=tt, p_=pap: e.activation(
                    out=at["xnfm"][:, c_, tt_ * 128:(tt_ + 1) * 128],
                    in_=p_, func=AF.Copy),
                    deps=[t_tr], inc="s")
                setf(pkey, t_ev)
                setf(bkey, t_ev)
                last_ev = t_ev
            setf("xnt", Tok("tr", sch.cnt["tr"]) if sch.cnt.get("tr") else None)
            setf(("xn_tt", tt), last_ev)

        ftl = ([("q", i, i) for i in range(8)]
               + [("k", i, i) for i in range(8)]
               + [("v", i, i + 8) for i in range(8)])
        last_proj_mm = None
        pend = []

        def flush_tp(item):
            tt_, qsl_, t_nm_, path_, stb_ = item
            for hl in range(2):
                pap, pkey, bkey = tsl.get()
                t_tr = sch.emit("tensor", lambda e, hl_=hl, p_=pap, q_=qsl_: e.transpose(
                    p_, at["qn_tm"][:, q_, hl_ * HD:(hl_ + 1) * HD], eye[:]),
                    deps=[t_nm_[hl], dep(pkey), dep(bkey)], inc="tr")
                if path_ == "q":
                    sc_ap, bi_ap = sb["lnh"][:, 2:3], sb["lnh"][:, 3:4]
                else:
                    sc_ap, bi_ap = sb["lnh"][:, 0:1], sb["lnh"][:, 1:2]
                t_ev = sch.emit("scalar", lambda e, hl_=hl, tt_=tt_, p_=pap,
                                b_=stb_, sc_=sc_ap, bi_=bi_ap: e.activation(
                    out=at["stage"][:, b_, hl_ * 2048 + tt_ * 128:
                                    hl_ * 2048 + tt_ * 128 + 128],
                    in_=p_, func=AF.Identity,
                    scale=sc_, bias=bi_),
                    deps=[t_tr, dep(("stage", stb_))], inc="s")
                setf(pkey, t_ev)
                setf(bkey, t_ev)
                setf(("stg_last", stb_), t_ev)
            setf(("qn_tm", qsl_), Tok("tr", sch.cnt["tr"]))

        for idx, (path, fti, wsrc_i) in enumerate(ftl[:n_ft]):
            wb = idx % 2
            stb = idx % 2
            if path == "q":
                wsrc = ap["WqS"][wsrc_i, :, :, :]
                gv, bv_bf = sb["gq"], sb["bqpre_bf"]
                brow_hbm = ap["bq_row"][:, fti * FT_W:(fti + 1) * FT_W]
            else:
                wsrc = ap["WkvS"][wsrc_i, :, :, :]
                gv, bv_bf = sb["gkv"], sb["bkvpre_bf"]
                brow_hbm = ap["bkv_row"][:, wsrc_i * FT_W:(wsrc_i + 1) * FT_W]
            t_rowd = sch.emit("sync", lambda e, d=at["rowbuf"][0:1, wb, :], s_=brow_hbm:
                              e.dma_start(out=d, in_=s_),
                              deps=[dep(("rowbuf", wb))], inc="srb%d" % wb, inc_by=16)
            t_w = sch.emit("gpsimd", lambda e, d=at["wsl"][:, wb, :, :], s_=wsrc:
                           e.dma_start(out=d, in_=s_),
                           deps=[dep(("wsl", wb))] + (persist_tok if idx == 0 else []),
                           inc="gw%d" % wb, inc_by=16)
            t_bm = None
            for c in range(NC):
                t_bm = sch.emit("tensor", lambda e, c_=c, wb_=wb, bv_=bv_bf: e.matmul(
                    ps[0:1, 4, 0:FT_W], bv_[:, c_:c_ + 1], at["wsl"][:, wb_, c_, :],
                    start=(c_ == 0), stop=(c_ == NC - 1)),
                    deps=[t_w, dep(("psb4",))] if c == 0 else [],
                    inc="pe" if c == NC - 1 else None)
            t_br = sch.emit("vector", lambda e, wb_=wb: e.tensor_add(
                at["brow"][0:1, wb_, :], ps[0:1, 4, 0:FT_W], at["rowbuf"][0:1, wb_, :]),
                deps=[t_bm, t_rowd, dep(("brow", wb))], inc="v")
            setf(("psb4",), t_br)
            setf(("rowbuf", wb), t_br)
            t_fold = None
            for c in range(NC):
                t_fold = sch.emit("vector", lambda e, c_=c, wb_=wb, gv_=gv: e.tensor_scalar(
                    out=at["wsl"][:, wb_, c_, :], in0=at["wsl"][:, wb_, c_, :],
                    scalar1=gv_[:, c_:c_ + 1], scalar2=None, op0=ALU.mult),
                    deps=[t_bm] if c == 0 else [], inc="v" if c == NC - 1 else None)
            for tt in range(NTT):
                sl = tt % 4
                bkkey = ("psb", sl) if sl < 2 else ("psb_av", sl - 2)
                for c in range(NC):
                    sch.emit("tensor", lambda e, c_=c, tt_=tt, sl_=sl, wb_=wb: e.matmul(
                        ps[:, sl_, 0:FT_W], at["xnfm"][:, c_, tt_ * 128:(tt_ + 1) * 128],
                        at["wsl"][:, wb_, c_, :], start=(c_ == 0), stop=False),
                        deps=([t_fold, dep(bkkey), dep(("xn_tt", tt))]
                              if c == 0 else []), inc=None)
                t_mm = sch.emit("tensor", lambda e, sl_=sl, wb_=wb: e.matmul(
                    ps[:, sl_, 0:FT_W], sb["ones_row"][:], at["brow"][0:1, wb_, :],
                    start=False, stop=True), deps=[t_br], inc="pe")
                last_proj_mm = t_mm
                setf(("brow", wb), t_mm)
                if path == "v":
                    t_ev = sch.emit("vector", lambda e, tt_=tt, sl_=sl, f_=fti, b_=stb:
                                    e.tensor_copy(
                        at["stage"][:, b_, tt_ * FT_W:(tt_ + 1) * FT_W],
                        ps[:, sl_, 0:FT_W]),
                        deps=[t_mm, dep(("stage", stb))], inc="v")
                    setf(bkkey, t_ev)
                    setf(("stg_last", stb), t_ev)
                else:
                    psq = ps[:, sl, 0:FT_W]
                    t_sw = sch.emit("vector", lambda e, psq_=psq: e.tensor_copy(
                        at["qsw"][:].rearrange("p (a b) -> p a b", b=2),
                        psq_.rearrange("p (a b) -> p a b", b=2)[:, :, ::-1]),
                        deps=[t_mm, dep("qsw")], inc="v")
                    rre2 = at["rre"][:, tt, :].unsqueeze(1).broadcast_to([128, 2, 128])
                    rse2 = at["rse"][:, tt, :].unsqueeze(1).broadcast_to([128, 2, 128])
                    t_m1 = sch.emit("vector", lambda e, psq_=psq, r_=rre2: e.tensor_tensor(
                        out=at["qrot"][:].rearrange("p (h f) -> p h f", f=128),
                        in0=psq_.rearrange("p (h f) -> p h f", f=128), in1=r_,
                        op=ALU.mult),
                        deps=[t_mm, dep("qrot"), dep(("rot", tt))], inc="v")
                    t_m2 = sch.emit("vector", lambda e, r_=rse2: e.tensor_tensor(
                        out=at["qsw"][:].rearrange("p (h f) -> p h f", f=128),
                        in0=at["qsw"][:].rearrange("p (h f) -> p h f", f=128), in1=r_,
                        op=ALU.mult), deps=[t_sw], inc="v")
                    t_add = sch.emit("vector", lambda e: e.tensor_add(
                        at["qrot"][:], at["qrot"][:], at["qsw"][:]),
                        deps=[t_m1, t_m2], inc="v")
                    setf(bkkey, t_m1)
                    setf("qsw", t_add)
                    t_sq2 = sch.emit("scalar", lambda e: e.activation(
                        out=sb["dummy_bf"][:, 0:FT_W], in_=at["qrot"][:],
                        func=AF.Square), deps=[t_add, dep("dummy")], inc="s")
                    hs = at["hvec"]
                    t_rs = sch.emit("vector", lambda e: e.reduce_sum(
                        hs[:, 0:2],
                        at["qrot"][:].rearrange("p (h f) -> p h f", f=HD),
                        axis=AX.X), deps=[t_add], inc="v")
                    t_rq = sch.emit("vector", lambda e: e.reduce_sum(
                        hs[:, 2:4],
                        sb["dummy_bf"][:, 0:FT_W].rearrange("p (h f) -> p h f", f=HD),
                        axis=AX.X), deps=[t_sq2], inc="v")
                    setf("dummy", t_rq)
                    t_a = sch.emit("vector", lambda e: e.tensor_scalar(
                        out=hs[:, 4:6], in0=hs[:, 0:2], scalar1=1.0 / HD, scalar2=None,
                        op0=ALU.mult), deps=[t_rs], inc="v")
                    t_b = sch.emit("vector", lambda e: e.tensor_mul(
                        hs[:, 6:8], hs[:, 0:2], hs[:, 4:6]), deps=[t_a], inc="v")
                    t_c = sch.emit("vector", lambda e: e.tensor_sub(
                        hs[:, 8:10], hs[:, 2:4], hs[:, 6:8]), deps=[t_b, t_rq], inc="v")
                    t_d = sch.emit("scalar", lambda e: e.activation(
                        out=hs[:, 10:12], in_=hs[:, 8:10], func=AF.Ln, scale=1.0 / HD,
                        bias=sb["cbias"][:, 0:1]), deps=[t_c], inc="s")
                    t_e = sch.emit("scalar", lambda e: e.activation(
                        out=hs[:, 12:14], in_=hs[:, 10:12], func=AF.Exp, scale=-0.5),
                        deps=[t_d], inc="s")
                    qsl = tt % 3
                    t_nm = [None, None]
                    for hl in range(2):
                        t_nm[hl] = sch.emit("vector", lambda e, hl_=hl, q_=qsl: e.tensor_scalar(
                            out=at["qn_tm"][:, q_, hl_ * HD:(hl_ + 1) * HD],
                            in0=at["qrot"][:, hl_ * HD:(hl_ + 1) * HD],
                            scalar1=hs[:, 4 + hl_:5 + hl_], scalar2=hs[:, 12 + hl_:13 + hl_],
                            op0=ALU.subtract, op1=ALU.mult),
                            deps=[t_e, dep(("qn_tm", qsl))], inc="v")
                    setf("qrot", t_nm[1])
                    pend.append((tt, qsl, t_nm, path, stb))
                    if len(pend) > 2:
                        flush_tp(pend.pop(0))
            while pend:
                flush_tp(pend.pop(0))
            t_last = dep(("stg_last", stb))
            for hl in range(2):
                h = fti * 2 + hl
                if path == "q":
                    t_d1 = sch.emit("sync", lambda e, h_=h, hl_=hl, b_=stb: e.dma_start(
                        out=qn_scr[h_, :, :],
                        in_=at["stage"][:, b_, hl_ * 2048:(hl_ + 1) * 2048]),
                        deps=[t_last], inc="ss%d" % stb, inc_by=16)
                elif path == "k":
                    t_d1 = sch.emit("sync", lambda e, h_=h, hl_=hl, b_=stb: e.dma_start(
                        out=kn_scr[h_, :, :],
                        in_=at["stage"][:, b_, hl_ * 2048:(hl_ + 1) * 2048]),
                        deps=[t_last], inc="ss%d" % stb, inc_by=16)
                else:
                    t_d1 = sch.emit("sync", lambda e, h_=h, hl_=hl, b_=stb: e.dma_start(
                        out=v_scr[h_, :, :, :],
                        in_=at["stage"][:, b_, :].rearrange(
                            "p (t j) -> p t j", j=FT_W)[:, :, hl_ * 128:(hl_ + 1) * 128]),
                        deps=[t_last], inc="ss%d" % stb, inc_by=16)
                setf(("stage", stb), t_d1)
            setf(("wsl", wb), last_proj_mm)
        setf("xnfm_war", last_proj_mm)

        if n_ft < 24:
            n_qg = 0
        for qg in range(n_qg):
            t_af_prev = dep("afm")
            SCB = (0, 1, 5)
            fin_pend = []

            def finish_h(item):
                h_, t_cast, t_sum_, t_av_ = item
                bh = SCB[h_ % 3]
                avb = 2 + h_ % 2
                t_bc = sch.emit("tensor", lambda e, b_=bh, h2=h_ % 2: e.matmul(
                    ps[:, b_, :], sb["ones_row"][:], at["sumb"][0:1, h2, :],
                    start=True, stop=True),
                    deps=[t_cast, dep(("psb", bh))], inc="pe")
                t_rc = sch.emit("vector", lambda e, b_=bh: e.reciprocal(
                    at["recip"][:], ps[:, b_, :]),
                    deps=[t_bc, dep("recip")], inc="v")
                setf(("psb", bh), t_rc)
                t_ae = sch.emit("vector", lambda e, h2=h_, a_=avb: e.tensor_tensor(
                    out=at["afm"][:, h2, :], in0=ps[:, a_, :], in1=at["recip"][:],
                    op=ALU.mult), deps=[t_av_, t_rc, t_af_prev], inc="v")
                setf("recip", t_ae)
                setf(("psb_av", h_ % 2), t_ae)
                setf(("sumb", h_ % 2), t_bc)
                setf(("afm_w", h_), t_ae)

            for h in range(NH):
                spill_done = ([Tok("ss0", sch.cnt.get("ss0", 0)),
                               Tok("ss1", sch.cnt.get("ss1", 0))]
                              if (qg, h) == (0, 0) else [])
                t_kn = sch.emit("sync", lambda e, h_=h: e.dma_start(
                    out=at["kn"][:], in_=kn_scr[h_, :, :]),
                    deps=[dep("kn")] + spill_done, inc="skn", inc_by=16)
                t_vv = sch.emit("sync", lambda e, h_=h: e.dma_start(
                    out=at["vv"][:], in_=v_scr[h_, :, :, :]),
                    deps=[dep("vv")], inc="svv", inc_by=16)
                t_qn = sch.emit("sync", lambda e, h_=h, qg_=qg: e.dma_start(
                    out=at["qnb"][:], in_=qn_scr[h_, :, qg_ * 512:(qg_ + 1) * 512]),
                    deps=[dep("qnb")], inc="sqn", inc_by=16)
                avb = 2 + h % 2
                t_sum = None
                t_av = None
                sa_pend = []

                def flush_sa(kt_, t_ex_):
                    nonlocal t_sum, t_av
                    t_sum = sch.emit("tensor", lambda e, k_=kt_: e.matmul(
                        ps[0:1, 4, :], sb["ones_col"][:], at["attT"][:, k_ % 3, :],
                        start=(k_ == 0), stop=(k_ == NTT - 1)),
                        deps=[t_ex_] + ([dep(("psb4",))] if kt_ == 0 else []), inc="pe")
                    t_av = sch.emit("tensor", lambda e, k_=kt_, a_=avb: e.matmul(
                        ps[:, a_, :], at["vv"][:, k_, :], at["attT"][:, k_ % 3, :],
                        start=(k_ == 0), stop=(k_ == NTT - 1)),
                        deps=[t_vv] + ([dep(("psb_av", h % 2))] if kt_ == 0 else []),
                        inc="pe")
                    setf(("attT", kt_ % 3), t_av)

                for kt in range(NTT):
                    scb = SCB[kt % 3]
                    t_sc = sch.emit("tensor", lambda e, kt_=kt, b_=scb: e.matmul(
                        ps[:, b_, :], at["kn"][:, kt_ * 128:(kt_ + 1) * 128],
                        at["qnb"][:], start=True, stop=True),
                        deps=[t_kn, t_qn, dep(("psb", scb))], inc="pe")
                    t_ex = sch.emit("scalar", lambda e, kt_=kt, b_=scb: e.activation(
                        out=at["attT"][:, kt_ % 3, :], in_=ps[:, b_, :], func=AF.Exp),
                        deps=[t_sc, dep(("attT", kt % 3))], inc="s")
                    setf(("psb", scb), t_ex)
                    sa_pend.append((kt, t_ex))
                    if len(sa_pend) > 2:
                        flush_sa(*sa_pend.pop(0))
                while sa_pend:
                    flush_sa(*sa_pend.pop(0))
                setf("kn", t_sc)
                setf("qnb", t_sc)
                setf("vv", t_av)
                t_cast = sch.emit("vector", lambda e, h2=h % 2: e.tensor_copy(
                    at["sumb"][0:1, h2, :], ps[0:1, 4, :]),
                    deps=[t_sum, dep(("sumb", h % 2))], inc="v")
                setf(("psb4",), t_cast)
                fin_pend.append((h, t_cast, t_sum, t_av))
                if len(fin_pend) > 1:
                    finish_h(fin_pend.pop(0))
            while fin_pend:
                finish_h(fin_pend.pop(0))
            for dq in range(4):
                for h in range(NH):
                    wslot = (dq * NH + h) % 3
                    t_wo = sch.emit("gpsimd", lambda e, h_=h, dq_=dq, w_=wslot: e.dma_start(
                        out=at["woring"][:, w_, :], in_=ap["WoS"][h_, dq_, :, :]),
                        deps=[dep(("wo", wslot))], inc="gwo%d" % wslot, inc_by=16)
                    t_om = None
                    for qt in range(4):
                        bk = (("psb", qt) if qt < 2 else ("psb_av", qt - 2))
                        t_om = sch.emit("tensor", lambda e, h_=h, qt_=qt, w_=wslot: e.matmul(
                            ps[:, qt_, :], at["afm"][:, h_, qt_ * 128:(qt_ + 1) * 128],
                            at["woring"][:, w_, :], start=(h_ == 0), stop=False),
                            deps=[t_wo, dep(("afm_w", h))]
                                 + ([dep(bk)] if h == 0 else []),
                            inc="pe" if qt == 3 else None)
                    setf(("wo", wslot), t_om)
                for qt in range(4):
                    t_ob = sch.emit("tensor", lambda e, qt_=qt, dq_=dq: e.matmul(
                        ps[:, qt_, :], sb["ones_row"][:],
                        sb["bo_row_bf"][0:1, dq_ * 512:(dq_ + 1) * 512],
                        start=False, stop=True), deps=[], inc="pe")
                    t_oe = sch.emit("scalar", lambda e, qt_=qt, dq_=dq: e.activation(
                        out=at["o_bf"][:, qt_, dq_ * 512:(dq_ + 1) * 512],
                        in_=ps[:, qt_, :], func=AF.Copy),
                        deps=[t_ob, dep(("o_bf", qt))], inc="s")
                    setf(("psb", qt) if qt < 2 else ("psb_av", qt - 2), t_oe)
                    t_last_o_ev = t_oe
                    setf(("o_ev", qt, dq), t_oe)
            setf("afm", Tok("pe", sch.cnt["pe"]))
            for qt in range(4):
                evs = [sch.dep(("o_ev", qt, dq)) for dq in range(4)]
                ob = at["o_bf"][:, qt, :]
                r0 = s * C + qg * 512 + qt * 128
                t_x2 = sch.emit("sync", lambda e, r_=qg * 512 + qt * 128, s_=s, q_=qt % 2:
                                e.dma_start(
                    out=at["xf"][:, q_, :], in_=ap["x"][s_, r_:r_ + 128, :]),
                    deps=[dep(("xf", qt % 2))], inc="sx%d" % (qt % 2), inc_by=16)
                ssq = vs.get(); ssum = vs.get()
                t_sq = sch.emit("scalar", lambda e, o_=ob, a=ssq: e.activation(
                    out=sb["dummy_bf"][:], in_=o_, func=AF.Square, accum_out=a),
                    deps=evs + [dep("dummy")], inc="s")
                t_sm = sch.emit("vector", lambda e, o_=ob, a=ssum: e.reduce_sum(
                    a, o_, axis=AX.X), deps=evs, inc="v")
                setf("dummy", t_sq)
                mu = vs.get(); e2 = vs.get(); tmp = vs.get(); lnt = vs.get(); r1 = vs.get()
                ta = sch.emit("vector", lambda e, a=mu, b=ssum: e.tensor_scalar(
                    out=a, in0=b, scalar1=1.0 / D, scalar2=None, op0=ALU.mult),
                    deps=[t_sm], inc="v")
                tb = sch.emit("vector", lambda e, a=tmp, b=ssum: e.scalar_tensor_tensor(
                    out=a, in0=b, scalar=1.0 / D, in1=b, op0=ALU.mult, op1=ALU.mult),
                    deps=[t_sm], inc="v")
                tc = sch.emit("vector", lambda e, a=e2, b=ssq, c=tmp: e.tensor_sub(a, b, c),
                              deps=[t_sq, tb], inc="v")
                td = sch.emit("scalar", lambda e, a=lnt, b=e2: e.activation(
                    out=a, in_=b, func=AF.Ln, scale=1.0 / D, bias=sb["cbias"][:, 0:1]), deps=[tc], inc="s")
                te = sch.emit("scalar", lambda e, a=r1, b=lnt: e.activation(
                    out=a, in_=b, func=AF.Exp, scale=-0.5), deps=[td], inc="s")
                tf = sch.emit("vector", lambda e, o_=ob, a=mu, b=r1: e.tensor_scalar(
                    out=at["of"][:], in0=o_, scalar1=a, scalar2=b,
                    op0=ALU.subtract, op1=ALU.mult), deps=[te, dep("of")], inc="v")
                tg = sch.emit("vector", lambda e, qt_=qt: e.tensor_add(
                    at["of"][:], at["of"][:], at["xf"][:, qt_ % 2, :]),
                    deps=[tf, t_x2], inc="v")
                setf(("xf", qt % 2), tg)
                setf(("o_bf", qt), tg)
                t_st = sch.emit("sync", lambda e, r_=r0: e.dma_start(
                    out=xmid_scr[r_:r_ + 128, :], in_=at["of"][:]),
                    deps=[tg], inc="sxd", inc_by=16)
                setf("of", t_st)
    return Tok("sxd", sch.cnt["sxd"]) if n_qg else None


def emit_ffn(nc, sch, sb, ap, es, persist_tok, xmid_tok, xmid_scr, comb_scr, out_ap,
             n_chunk=4):
    vs = VecSlots(sb["vec"], 64)
    ft = {}
    def sbt(name, shape, dt=F32):
        ft[name] = es.enter_context(nc.sbuf_tensor("f_" + name, shape, dt))
        return ft[name]
    sbt("xn2", [128, NC, CHUNK], BF16)
    sbt("lofm", [128, NC, 128], BF16)
    sbt("yacc", [128, 8, D], F32)
    sbt("scr", [128, 2, 6, CHUNK], BF16)
    sbt("wring", [128, 4, 4096], BF16)
    sbt("gwhi", [128, NC, N_EXP], BF16)
    sbt("gwlo", [128, NC, N_EXP], BF16)
    sbt("gwf", [128, NC, N_EXP], F32)
    sbt("scores", [128, 8, N_EXP], F32)
    sbt("m8", [128, 8], F32)
    sbt("mask", [128, N_EXP], F32)
    sbt("combT", [32, 128], BF16)
    sbt("crow", [1, 2, CHUNK], BF16)
    sbt("cb", [128, CHUNK], BF16)
    sbt("xn2f", [128, D], F32)
    sbt("gs", [128, CHUNK], BF16)
    sbt("us", [128, CHUNK], BF16)
    sbt("xm", [128, D], F32)

    ps = es.enter_context(nc.psum_tensor("psF", [128, 6, 512], F32))
    pst = es.enter_context(nc.psum_tensor("psFT", [128, 2, 1024], BF16))
    tsl = TSlots(pst)
    eye = sb["eye"]
    dep, setf = sch.dep, sch.setf
    g_ps = ps[:, 0:2, :].rearrange("p a b -> p (a b)")
    u_ps = ps[:, 2:4, :].rearrange("p a b -> p (a b)")

    t_g1 = sch.emit("gpsimd", lambda e: e.dma_start(out=ft["gwhi"][:], in_=ap["gwS"]),
                    deps=[xmid_tok], inc="gm1", inc_by=16)
    t_g2 = sch.emit("sync", lambda e: e.dma_start(out=ft["gwf"][:], in_=ap["gwS"]),
                    deps=[xmid_tok], inc="gm2", inc_by=16)
    t_g3 = sch.emit("vector", lambda e: e.tensor_sub(ft["gwlo"][:], ft["gwf"][:], ft["gwhi"][:]),
                    deps=[t_g1, t_g2], inc="v")

    for k in range(n_chunk):
        r0 = k * CHUNK
        def f0_a(tt):
            sl = tt % 2
            hi_v = ft["scr"][:, 0, sl * 2:sl * 2 + 2, :].rearrange("p a b -> p (a b)")
            lo_v = ft["scr"][:, 1, sl * 2:sl * 2 + 2, :].rearrange("p a b -> p (a b)")
            xb = ft["xm"][:]
            t_x = sch.emit("sync", lambda e, d=xb, r_=r0 + tt * 128: e.dma_start(
                out=d, in_=xmid_scr[r_:r_ + 128, :]),
                deps=[dep("fxm"), xmid_tok], inc="sxm", inc_by=16)
            rc, bc, stoks = emit_ln_stats(sch, sb, vs, xb, D, [t_x], False)
            t_xf = sch.emit("vector", lambda e, xb_=xb, rc_=rc, bc_=bc: e.tensor_scalar(
                out=ft["xn2f"][:], in0=xb_, scalar1=rc_, scalar2=bc_,
                op0=ALU.mult, op1=ALU.add), deps=stoks + [dep("xn2f")], inc="v")
            setf("fxm", t_xf)
            t_hi = sch.emit("vector", lambda e, h_=hi_v: e.tensor_copy(
                h_, ft["xn2f"][:]),
                deps=[t_xf, dep(("act", 0, sl * 2)), dep(("act", 0, sl * 2 + 1))],
                inc="v")
            t_lo = sch.emit("vector", lambda e, h_=hi_v, l_=lo_v: e.tensor_sub(
                l_, ft["xn2f"][:], h_),
                deps=[t_hi, dep(("act", 1, sl * 2)), dep(("act", 1, sl * 2 + 1))],
                inc="v")
            setf("xn2f", t_lo)
            return (tt, sl, t_hi, t_lo, hi_v, lo_v)

        def f0_b(item):
            tt, sl, t_hi, t_lo, hi_v, lo_v = item
            for c in range(NC):
                pap, pkey, bkey = tsl.get()
                t_tr = sch.emit("tensor", lambda e, c_=c, p_=pap, h_=hi_v: e.transpose(
                    p_, h_[:, c_ * 128:(c_ + 1) * 128],
                    eye[:]), deps=[t_hi, dep(pkey), dep(bkey), dep("fwar")], inc="tr")
                t_ev = sch.emit("scalar", lambda e, c_=c, tt_=tt, p_=pap: e.activation(
                    out=ft["xn2"][:, c_, tt_ * 128:(tt_ + 1) * 128],
                    in_=p_, func=AF.Copy),
                    deps=[t_tr], inc="s")
                setf(pkey, t_ev)
                setf(bkey, t_ev)
                pap2, pkey2, bkey2 = tsl.get()
                t_tr2 = sch.emit("tensor", lambda e, c_=c, p_=pap2, l_=lo_v: e.transpose(
                    p_, l_[:, c_ * 128:(c_ + 1) * 128],
                    eye[:]), deps=[t_lo, dep(pkey2), dep(bkey2)], inc="tr")
                t_ev2 = sch.emit("scalar", lambda e, c_=c, p_=pap2: e.activation(
                    out=ft["lofm"][:, c_, :], in_=p_,
                    func=AF.Copy), deps=[t_tr2, dep(("lofm", c))], inc="s")
                setf(pkey2, t_ev2)
                setf(bkey2, t_ev2)
                setf(("xn2_tt", tt), t_ev)
                setf(("lofm_ev", c), t_ev2)
            trtok = Tok("tr", sch.cnt["tr"])
            for jj in (sl * 2, sl * 2 + 1):
                setf(("act", 0, jj), trtok)
                setf(("act", 1, jj), trtok)
            t_gm = None
            for c in range(NC):
                t_gm = sch.emit("tensor", lambda e, c_=c, tt_=tt: e.matmul(
                    ps[:, 4, 0:N_EXP], ft["xn2"][:, c_, tt_ * 128:(tt_ + 1) * 128],
                    ft["gwhi"][:, c_, :], start=(c_ == 0), stop=False),
                    deps=([t_g1, dep(("ps", 4)), dep(("xn2_tt", tt))] if c == 0 else []),
                    inc=None)
            for c in range(NC):
                t_gm = sch.emit("tensor", lambda e, c_=c: e.matmul(
                    ps[:, 4, 0:N_EXP], ft["lofm"][:, c_, :], ft["gwhi"][:, c_, :],
                    start=False, stop=False), deps=[dep(("lofm_ev", c))], inc=None)
            for c in range(NC):
                t_gm = sch.emit("tensor", lambda e, c_=c, tt_=tt: e.matmul(
                    ps[:, 4, 0:N_EXP], ft["xn2"][:, c_, tt_ * 128:(tt_ + 1) * 128],
                    ft["gwlo"][:, c_, :], start=False, stop=(c_ == NC - 1)),
                    deps=[t_g3] if c == 0 else [], inc="pe" if c == NC - 1 else None)
            for c in range(NC):
                setf(("lofm", c), t_gm)
            t_s1 = sch.emit("scalar", lambda e, tt_=tt: e.activation(
                out=ft["mask"][:], in_=ps[:, 4, 0:N_EXP], func=AF.Exp, scale=-1.0),
                deps=[t_gm, dep("fmask")], inc="s")
            setf(("ps", 4), t_s1)
            t_s2 = sch.emit("vector", lambda e: e.tensor_scalar(
                out=ft["mask"][:], in0=ft["mask"][:], scalar1=1.0, scalar2=None,
                op0=ALU.add), deps=[t_s1], inc="v")
            t_s3 = sch.emit("vector", lambda e, tt_=tt: e.reciprocal(
                ft["scores"][:, tt_, :], ft["mask"][:]), deps=[t_s2], inc="v")
            t_m8 = sch.emit("vector", lambda e, tt_=tt: e.max(
                out=ft["m8"][:], in_=ft["scores"][:, tt_, :]), deps=[t_s3], inc="v")
            t_mk = sch.emit("vector", lambda e, tt_=tt: e.tensor_single_scalar(
                out=ft["mask"][:], in_=ft["scores"][:, tt_, :],
                scalar=ft["m8"][:, 5:6], op=ALU.is_ge), deps=[t_m8], inc="v")
            setf("fmask", t_mk)
            t_ms = sch.emit("vector", lambda e, tt_=tt: e.tensor_mul(
                ft["mask"][:], ft["mask"][:], ft["scores"][:, tt_, :]), deps=[t_mk], inc="v")
            ssum = vs.get(); rw = vs.get()
            t_sw = sch.emit("vector", lambda e, a=ssum: e.reduce_sum(
                a, ft["mask"][:], axis=AX.X), deps=[t_ms], inc="v")
            t_rw = sch.emit("vector", lambda e, a=ssum: e.tensor_scalar(
                out=a, in0=a, scalar1=1e-20, scalar2=None, op0=ALU.add), deps=[t_sw], inc="v")
            t_rw2 = sch.emit("vector", lambda e, a=ssum, b=rw: e.reciprocal(b, a),
                             deps=[t_rw], inc="v")
            t_cb = sch.emit("vector", lambda e, b=rw: e.tensor_scalar(
                out=ft["mask"][:], in0=ft["mask"][:], scalar1=b, scalar2=None,
                op0=ALU.mult), deps=[t_rw2], inc="v")
            t_ct = sch.emit("tensor", lambda e: e.transpose(
                ps[0:32, 4, 0:128], ft["mask"][:, 0:N_EXP], sb["eye_f"][:]),
                deps=[t_cb, dep(("ps", 4)), dep("combT_r")], inc="tr")
            t_ce = sch.emit("vector", lambda e: e.tensor_copy(
                ft["combT"][:], ps[0:32, 4, 0:128]), deps=[t_ct, dep("combT")], inc="v")
            setf(("ps", 4), t_ce)
            setf("fmask", t_ce)
            t_cst = sch.emit("sync", lambda e, tt_=tt, k_=k: e.dma_start(
                out=comb_scr[:, tt_ * 128:(tt_ + 1) * 128], in_=ft["combT"][:]),
                deps=[t_ce], inc="scmb", inc_by=16)
            setf("combT", t_cst)
            setf(("combw", tt), t_cst)

        prev_a = f0_a(0)
        for tt in range(1, 8):
            cur_a = f0_a(tt)
            f0_b(prev_a)
            prev_a = cur_a
        f0_b(prev_a)
        emit_ffn_shared(nc, sch, sb, ft, ap, ps, k)
        emit_ffn_experts(nc, sch, sb, ft, ap, ps, k, comb_scr)
        for tt in range(8):
            ya = ft["yacc"][:, tt, :]
            ydep = [dep(("yacc", tt))]
            xb = ft["xm"][:]
            t_x = sch.emit("sync", lambda e, d=xb, r_=r0 + tt * 128: e.dma_start(
                out=d, in_=xmid_scr[r_:r_ + 128, :]),
                deps=[dep("fxm")], inc="sxm", inc_by=16)
            ssq = vs.get(); ssum = vs.get()
            t_sq = sch.emit("scalar", lambda e, y_=ya, a=ssq: e.activation(
                out=sb["dummy_bf"][:], in_=y_, func=AF.Square, accum_out=a),
                deps=ydep + [dep("dummy")], inc="s")
            t_sm = sch.emit("vector", lambda e, y_=ya, a=ssum: e.reduce_sum(
                a, y_, axis=AX.X), deps=ydep, inc="v")
            setf("dummy", t_sq)
            mu = vs.get(); e2 = vs.get(); tmp = vs.get(); lnt = vs.get(); r1 = vs.get()
            ta = sch.emit("vector", lambda e, a=mu, b=ssum: e.tensor_scalar(
                out=a, in0=b, scalar1=1.0 / D, scalar2=None, op0=ALU.mult),
                deps=[t_sm], inc="v")
            tb = sch.emit("vector", lambda e, a=tmp, b=ssum: e.scalar_tensor_tensor(
                out=a, in0=b, scalar=1.0 / D, in1=b, op0=ALU.mult, op1=ALU.mult),
                deps=[t_sm], inc="v")
            tc = sch.emit("vector", lambda e, a=e2, b=ssq, c=tmp: e.tensor_sub(a, b, c),
                          deps=[t_sq, tb], inc="v")
            td = sch.emit("scalar", lambda e, a=lnt, b=e2: e.activation(
                out=a, in_=b, func=AF.Ln, scale=1.0 / D, bias=sb["cbias"][:, 0:1]), deps=[tc], inc="s")
            te = sch.emit("scalar", lambda e, a=r1, b=lnt: e.activation(
                out=a, in_=b, func=AF.Exp, scale=-0.5), deps=[td], inc="s")
            tf = sch.emit("vector", lambda e, y_=ya, a=mu, b=r1: e.tensor_scalar(
                out=ft["xn2f"][:], in0=y_, scalar1=a, scalar2=b,
                op0=ALU.subtract, op1=ALU.mult), deps=[te, dep("xn2f")], inc="v")
            tg = sch.emit("vector", lambda e, xb_=xb: e.tensor_add(
                ft["xn2f"][:], ft["xn2f"][:], xb_), deps=[tf, t_x], inc="v")
            setf("fxm", tg)
            setf(("yacc", tt), tg)
            t_st = sch.emit("sync", lambda e, r_=r0 + tt * 128: e.dma_start(
                out=out_ap[r_ // C, (r_ % C):(r_ % C) + 128, :], in_=ft["xn2f"][:]),
                deps=[tg], inc="sout", inc_by=16)
            setf("xn2f", t_st)
        setf("fwar", Tok("pe", sch.cnt["pe"]))


def emit_ffn_shared(nc, sch, sb, ft, ap, ps, k):
    dep, setf = sch.dep, sch.setf
    def ring_slot():
        s = sch.ring % 4
        sch.ring += 1
        return s
    for half in range(2):
        for third in range(3):
            sgs = ring_slot(); sus = ring_slot()
            t_wg = sch.emit("gpsimd", lambda e, h_=half, t_=third, s_=sgs: e.dma_start(
                out=ft["wring"][:, s_, :].rearrange("p (c f) -> p c f", f=FT_W),
                in_=ap["SgS"][h_, :, :, t_ * FT_W:(t_ + 1) * FT_W]),
                deps=[dep(("wr", sgs))], inc="gwr%d" % sgs, inc_by=16)
            t_wu = sch.emit("gpsimd", lambda e, h_=half, t_=third, s_=sus: e.dma_start(
                out=ft["wring"][:, s_, :].rearrange("p (c f) -> p c f", f=FT_W),
                in_=ap["SuS"][h_, :, :, t_ * FT_W:(t_ + 1) * FT_W]),
                deps=[dep(("wr", sus))], inc="gwr%d" % sus, inc_by=16)
            wg_v = ft["wring"][:, sgs, :].rearrange("p (c f) -> p c f", f=FT_W)
            wu_v = ft["wring"][:, sus, :].rearrange("p (c f) -> p c f", f=FT_W)
            t_gm = t_um = None
            for fs in range(2):
                lf = third * 2 + fs
                for n2 in range(2):
                    for c in range(NC):
                        t_gm = sch.emit("tensor", lambda e, c_=c, n_=n2, f_=fs, w_=wg_v:
                                        e.matmul(
                            ps[:, n_, :], w_[:, c_, f_ * 128:(f_ + 1) * 128],
                            ft["xn2"][:, c_, n_ * 512:(n_ + 1) * 512],
                            start=(c_ == 0), stop=(c_ == NC - 1)),
                            deps=([t_wg, dep(("ps", n2))]
                                  + [dep(("xn2_tt", t)) for t in range(n2 * 4, n2 * 4 + 4)]
                                  if c == 0 else []),
                            inc="pe" if c == NC - 1 else None)
                    t_sil = sch.emit("scalar", lambda e, n_=n2: e.activation(
                        out=ft["gs"][:, n_ * 512:(n_ + 1) * 512], in_=ps[:, n_, :],
                        func=AF.Silu), deps=[t_gm, dep(("gs", n2))], inc="s")
                    setf(("ps", n2), t_sil)
                    for c in range(NC):
                        t_um = sch.emit("tensor", lambda e, c_=c, n_=n2, f_=fs, w_=wu_v:
                                        e.matmul(
                            ps[:, 2 + n_, :], w_[:, c_, f_ * 128:(f_ + 1) * 128],
                            ft["xn2"][:, c_, n_ * 512:(n_ + 1) * 512],
                            start=(c_ == 0), stop=(c_ == NC - 1)),
                            deps=([t_wu, dep(("ps", 2 + n2))] if c == 0 else []),
                            inc="pe" if c == NC - 1 else None)
                    t_ue = sch.emit("vector", lambda e, n_=n2: e.tensor_copy(
                        ft["us"][:, n_ * 512:(n_ + 1) * 512], ps[:, 2 + n_, :]),
                        deps=[t_um, dep(("us", n2))], inc="v")
                    setf(("ps", 2 + n2), t_ue)
                    t_ss = sch.emit("vector", lambda e, h_=half, l_=lf, n_=n2: e.tensor_mul(
                        ft["scr"][:, h_, l_, n_ * 512:(n_ + 1) * 512],
                        ft["gs"][:, n_ * 512:(n_ + 1) * 512],
                        ft["us"][:, n_ * 512:(n_ + 1) * 512]),
                        deps=[t_sil, t_ue, dep(("act", half, lf))], inc="v")
                    setf(("gs", n2), t_ss)
                    setf(("us", n2), t_ss)
                    setf(("ss", half, lf), t_ss)
            setf(("wr", sgs), t_gm)
            setf(("wr", sus), t_um)
    ss_deps = [dep(("ss", hh, j)) for hh in range(2) for j in range(6)]
    for dh in range(2):
        pieces = []
        for p3 in range(3):
            sl = ring_slot()
            t_sd = sch.emit("gpsimd", lambda e, h_=dh, p_=p3, s_=sl: e.dma_start(
                out=ft["wring"][:, s_, :].rearrange("p (c f) -> p c f", f=1024),
                in_=ap["SdS"][h_, :, p_ * 4:(p_ + 1) * 4, :]),
                deps=[dep(("wr", sl))], inc="gwr%d" % sl, inc_by=16)
            pieces.append((sl, t_sd))
        for tt in range(8):
            for d2 in range(2):
                dq = dh * 2 + d2
                psl = 4 + (tt * 2 + d2) % 2
                t_dm = None
                for fcg in range(12):
                    sl, t_sd = pieces[fcg // 4]
                    sd_v = ft["wring"][:, sl, :].rearrange("p (c f) -> p c f", f=1024)
                    t_dm = sch.emit("tensor", lambda e, f_=fcg, tt_=tt, d_=d2,
                                    sl_=psl, w_=sd_v: e.matmul(
                        ps[:, sl_, :],
                        ft["scr"][:, f_ // 6, f_ % 6, tt_ * 128:(tt_ + 1) * 128],
                        w_[:, f_ % 4, d_ * 512:(d_ + 1) * 512],
                        start=(f_ == 0), stop=(f_ == 11)),
                        deps=([t_sd, dep(("ps", psl))] + ss_deps if fcg == 0 else [t_sd]),
                        inc="pe" if fcg == 11 else None)
                t_ye = sch.emit("vector", lambda e, tt_=tt, q_=dq, sl_=psl: e.tensor_copy(
                    ft["yacc"][:, tt_, q_ * 512:(q_ + 1) * 512], ps[:, sl_, :]),
                    deps=[t_dm, dep(("yacc", tt))], inc="v")
                setf(("ps", psl), t_ye)
                setf(("yaccw", tt, dq), t_ye)
        for sl, _ in pieces:
            setf(("wr", sl), Tok("pe", sch.cnt["pe"]))
    pe_now = Tok("pe", sch.cnt["pe"])
    for hh in range(2):
        for j in range(6):
            setf(("act", hh, j), pe_now)


def emit_ffn_experts(nc, sch, sb, ft, ap, ps, k, comb_scr):
    dep, setf = sch.dep, sch.setf
    def ring_slot():
        s = sch.ring % 4
        sch.ring += 1
        return s
    for e_i in range(N_EXP):
        ab = e_i % 2
        cw = [dep(("combw", t)) for t in range(8)] if e_i == 0 else []
        t_cr = sch.emit("sync", lambda e, e_=e_i, b_=ab: e.dma_start(
            out=ft["crow"][0:1, b_, :], in_=comb_scr[e_:e_ + 1, :]),
            deps=[dep(("crow", ab))] + cw, inc="scr%d" % ab, inc_by=16)
        crow = ft["crow"][0:1, ab, :]
        t_c1 = sch.emit("tensor", lambda e, r_=crow: e.matmul(
            ps[:, 5, :], sb["ones_row"][:], r_[:, 0:512], start=True, stop=True),
            deps=[dep(("ps", 5)), t_cr], inc="pe")
        t_c1e = sch.emit("vector", lambda e: e.tensor_copy(ft["cb"][:, 0:512], ps[:, 5, :]),
                         deps=[t_c1, dep("cb")], inc="v")
        t_c2 = sch.emit("tensor", lambda e, r_=crow: e.matmul(
            ps[:, 5, :], sb["ones_row"][:], r_[:, 512:1024], start=True, stop=True),
            deps=[t_c1e], inc="pe")
        t_c2e = sch.emit("vector", lambda e: e.tensor_copy(ft["cb"][:, 512:1024], ps[:, 5, :]),
                         deps=[t_c2], inc="v")
        setf(("ps", 5), t_c2e)
        setf(("crow", ab), Tok("pe", sch.cnt["pe"]))
        setf("combT_r", Tok("pe", sch.cnt["pe"]))
        cb_tok = t_c2e
        for third in range(3):
            sgs = ring_slot(); sus = ring_slot()
            t_wg = sch.emit("gpsimd", lambda e, e_=e_i, t_=third, s_=sgs: e.dma_start(
                out=ft["wring"][:, s_, :].rearrange("p (c f) -> p c f", f=FT_W),
                in_=ap["WgS"][e_, :, :, t_ * FT_W:(t_ + 1) * FT_W]),
                deps=[dep(("wr", sgs))], inc="gwr%d" % sgs, inc_by=16)
            t_wu = sch.emit("gpsimd", lambda e, e_=e_i, t_=third, s_=sus: e.dma_start(
                out=ft["wring"][:, s_, :].rearrange("p (c f) -> p c f", f=FT_W),
                in_=ap["WuS"][e_, :, :, t_ * FT_W:(t_ + 1) * FT_W]),
                deps=[dep(("wr", sus))], inc="gwr%d" % sus, inc_by=16)
            wg_v = ft["wring"][:, sgs, :].rearrange("p (c f) -> p c f", f=FT_W)
            wu_v = ft["wring"][:, sus, :].rearrange("p (c f) -> p c f", f=FT_W)
            t_gm = t_um = None
            for fs in range(2):
                lf = third * 2 + fs
                for n2 in range(2):
                    for c in range(NC):
                        t_gm = sch.emit("tensor", lambda e, c_=c, n_=n2, f_=fs, w_=wg_v:
                                        e.matmul(
                            ps[:, n_, :], w_[:, c_, f_ * 128:(f_ + 1) * 128],
                            ft["xn2"][:, c_, n_ * 512:(n_ + 1) * 512],
                            start=(c_ == 0), stop=(c_ == NC - 1)),
                            deps=([t_wg, dep(("ps", n2))] if c == 0 else []),
                            inc="pe" if c == NC - 1 else None)
                    t_sil = sch.emit("scalar", lambda e, n_=n2: e.activation(
                        out=ft["gs"][:, n_ * 512:(n_ + 1) * 512], in_=ps[:, n_, :],
                        func=AF.Silu), deps=[t_gm, dep(("gs", n2))], inc="s")
                    setf(("ps", n2), t_sil)
                    for c in range(NC):
                        t_um = sch.emit("tensor", lambda e, c_=c, n_=n2, f_=fs, w_=wu_v:
                                        e.matmul(
                            ps[:, 2 + n_, :], w_[:, c_, f_ * 128:(f_ + 1) * 128],
                            ft["xn2"][:, c_, n_ * 512:(n_ + 1) * 512],
                            start=(c_ == 0), stop=(c_ == NC - 1)),
                            deps=([t_wu, dep(("ps", 2 + n2))] if c == 0 else []),
                            inc="pe" if c == NC - 1 else None)
                    t_ue = sch.emit("vector", lambda e, n_=n2: e.tensor_tensor(
                        out=ft["us"][:, n_ * 512:(n_ + 1) * 512], in0=ps[:, 2 + n_, :],
                        in1=ft["cb"][:, n_ * 512:(n_ + 1) * 512], op=ALU.mult),
                        deps=[t_um, cb_tok, dep(("us", n2))], inc="v")
                    setf(("ps", 2 + n2), t_ue)
                    t_ac = sch.emit("vector", lambda e, a_=ab, l_=lf, n_=n2: e.tensor_mul(
                        ft["scr"][:, a_, l_, n_ * 512:(n_ + 1) * 512],
                        ft["gs"][:, n_ * 512:(n_ + 1) * 512],
                        ft["us"][:, n_ * 512:(n_ + 1) * 512]),
                        deps=[t_sil, t_ue, dep(("act", ab, lf))], inc="v")
                    setf(("gs", n2), t_ac)
                    setf(("us", n2), t_ac)
                    setf(("actw", ab, lf), t_ac)
            setf(("wr", sgs), t_gm)
            setf(("wr", sus), t_um)
        wd_slots = []
        for p3 in range(3):
            sl = ring_slot()
            t_wd = sch.emit("gpsimd", lambda e, e_=e_i, p_=p3, s_=sl: e.dma_start(
                out=ft["wring"][:, s_, :].rearrange("p (c f) -> p c f", f=D),
                in_=ap["WdS"][e_, :, p_ * 2:(p_ + 1) * 2, :]),
                deps=[dep(("wr", sl))], inc="gwr%d" % sl, inc_by=16)
            wd_slots.append((sl, t_wd))
        act_deps = [dep(("actw", ab, j)) for j in range(6)]
        for tt in range(8):
            for dq in range(4):
                psl = 4 + (tt * 4 + dq) % 2
                t_dm = None
                for fc in range(6):
                    sl, t_wd = wd_slots[fc // 2]
                    wd_v = ft["wring"][:, sl, :].rearrange("p (c f) -> p c f", f=D)
                    t_dm = sch.emit("tensor", lambda e, f_=fc, tt_=tt, dq_=dq,
                                    sl_=psl, w_=wd_v, a_=ab: e.matmul(
                        ps[:, sl_, :],
                        ft["scr"][:, a_, f_, tt_ * 128:(tt_ + 1) * 128],
                        w_[:, f_ % 2, dq_ * 512:(dq_ + 1) * 512],
                        start=(f_ == 0), stop=(f_ == 5)),
                        deps=([t_wd, dep(("ps", psl))] + act_deps if fc == 0 else [t_wd]),
                        inc="pe" if fc == 5 else None)
                t_ye = sch.emit("vector", lambda e, tt_=tt, dq_=dq, sl_=psl: e.tensor_add(
                    ft["yacc"][:, tt_, dq_ * 512:(dq_ + 1) * 512],
                    ft["yacc"][:, tt_, dq_ * 512:(dq_ + 1) * 512],
                    ps[:, sl_, :]), deps=[t_dm, dep(("yaccw", tt, dq))], inc="v")
                setf(("ps", psl), t_ye)
                setf(("yaccw", tt, dq), t_ye)
                if e_i == N_EXP - 1:
                    setf(("yacc", tt), t_ye)
        pe_now = Tok("pe", sch.cnt["pe"])
        for sl, _ in wd_slots:
            setf(("wr", sl), pe_now)
        for j in range(6):
            setf(("act", ab, j), pe_now)


def build_program(n_seq=SEQ_PER_CORE, do_attn=True, do_ffn=True, taps=False,
                  n_qg=4, n_ft=24, n_tt=NTT):
    nc = bass.Bass("TRN2", target_bir_lowering=False)
    ap = {}
    def din(name, shape, dt=F32):
        ap[name] = nc.dram_tensor(name, shape, dt, kind="ExternalInput").ap()
    din("x", [SEQ_PER_CORE, C, D])
    din("WqS", [8, 128, NC, FT_W]); din("WkvS", [16, 128, NC, FT_W])
    din("WoS", [NH, 4, 128, 512])
    din("gwS", [128, NC, N_EXP])
    din("WgS", [N_EXP, 128, NC, F_EXP]); din("WuS", [N_EXP, 128, NC, F_EXP])
    din("WdS", [N_EXP, 128, 6, D])
    din("SgS", [2, 128, NC, 768]); din("SuS", [2, 128, NC, 768])
    din("SdS", [2, 128, 12, 1024])
    din("gq", [128, NC]); din("bqpre", [128, NC]); din("gkv", [128, NC]); din("bkvpre", [128, NC])
    din("bq_row", [1, D]); din("bkv_row", [1, 2 * D]); din("bo_row", [1, D])
    din("lnh_g", [128, 1]); din("lnh_b", [128, 1])
    din("c_eye", [128, 128], BF16); din("c_eye_f", [128, 128], F32)
    din("c_ones_row", [1, 128], BF16); din("c_ones_col", [128, 1], BF16)
    din("c_cos", [128, NTT, HD2]); din("c_sin", [128, NTT, HD2])

    out_ap = nc.dram_tensor("out", [SEQ_PER_CORE, C, D], F32, kind="ExternalOutput").ap()
    scr_kind = "ExternalOutput" if taps else "Internal"
    qn_scr = nc.dram_tensor("qn_scr", [NH, 128, C], BF16, kind=scr_kind).ap()
    kn_scr = nc.dram_tensor("kn_scr", [NH, 128, C], BF16, kind=scr_kind).ap()
    v_scr = nc.dram_tensor("v_scr", [NH, 128, NTT, 128], BF16, kind=scr_kind).ap()
    xmid_scr = nc.dram_tensor("xmid_scr", [SEQ_PER_CORE * C, D], F32, kind=scr_kind).ap()
    comb_scr = nc.dram_tensor("comb_scr", [N_EXP, CHUNK], BF16, kind="Internal").ap()

    with ExitStack() as es:
        sch = Sched(nc, es)
        sb = {}
        def sbt(name, shape, dt=F32):
            sb[name] = es.enter_context(nc.sbuf_tensor("sb_" + name, shape, dt))
            return sb[name]
        sbt("eye", [128, 128], BF16); sbt("eye_f", [128, 128], F32)
        sbt("ones_row", [1, 128], BF16); sbt("ones_col", [128, 1], BF16)
        sbt("gq", [128, NC]); sbt("bqpre", [128, NC]); sbt("gkv", [128, NC]); sbt("bkvpre", [128, NC])
        sbt("bqpre_bf", [128, NC], BF16); sbt("bkvpre_bf", [128, NC], BF16)
        sbt("bo_row_bf", [1, D], BF16)
        sbt("lnh", [128, 8])
        sbt("cbias", [128, 2])
        sbt("vec", [128, 64])
        sbt("mrot", [128, NTT])
        sbt("dummy_bf", [128, 2048], BF16)

        with nc.Block() as block:
            ld = []
            def pload(dst, src, cast=False):
                eng = "gpsimd" if cast else "sync"
                ld.append(sch.emit(eng, lambda e, d=dst, s=src: e.dma_start(out=d, in_=s),
                                   inc="gm0" if cast else "sper", inc_by=16))
            pload(sb["eye"][:], ap["c_eye"]); pload(sb["eye_f"][:], ap["c_eye_f"])
            pload(sb["ones_row"][:], ap["c_ones_row"]); pload(sb["ones_col"][:], ap["c_ones_col"])
            for n in ("gq", "bqpre", "gkv", "bkvpre"):
                pload(sb[n][:], ap[n])
            pload(sb["bo_row_bf"][:], ap["bo_row"], cast=True)
            pload(sb["lnh"][:, 0:1], ap["lnh_g"]); pload(sb["lnh"][:, 1:2], ap["lnh_b"])
            allld = list(ld)
            t1 = sch.emit("vector", lambda e: e.tensor_copy(sb["bqpre_bf"][:], sb["bqpre"][:]),
                          deps=allld, inc="v")
            t2 = sch.emit("vector", lambda e: e.tensor_copy(sb["bkvpre_bf"][:], sb["bkvpre"][:]),
                          inc="v")
            t4 = sch.emit("vector", lambda e: e.tensor_scalar(
                out=sb["lnh"][:, 2:3], in0=sb["lnh"][:, 0:1], scalar1=S128, scalar2=None,
                op0=ALU.mult), inc="v")
            t5 = sch.emit("vector", lambda e: e.tensor_scalar(
                out=sb["lnh"][:, 3:4], in0=sb["lnh"][:, 1:2], scalar1=S128, scalar2=None,
                op0=ALU.mult), inc="v")
            t6 = sch.emit("vector", lambda e: e.memset(sb["cbias"][:, 0:1], EPS), inc="v")
            t7 = sch.emit("vector", lambda e: e.memset(sb["cbias"][:, 1:2], math.log(MROT)),
                          inc="v")
            persist_tok = [t1, t2, t4, t5, t6, t7] + allld

            xmid_tok = None
            if do_attn:
                with ExitStack() as aes:
                    xmid_tok = emit_attention(nc, sch, sb, ap, aes, persist_tok,
                                              qn_scr, kn_scr, v_scr, xmid_scr, n_seq,
                                              n_qg=n_qg, n_ft=n_ft, n_tt=n_tt)
                    sch.replay(block)
            if do_ffn:
                with ExitStack() as fes:
                    emit_ffn(nc, sch, sb, ap, fes, persist_tok, xmid_tok, xmid_scr,
                             comb_scr, out_ap, n_chunk=n_seq * 2)
                    sch.replay(block)
            sch.wait_only("sync", sch.final_toks())
            sch.replay(block)
    return nc


def kernel(**inputs):
    sh, xs = prep_host(inputs)
    nc = build_program()
    in_maps = []
    for c in range(N_CORES):
        m = dict(sh)
        m["x"] = xs[c]
        in_maps.append(m)
    res = run_bass_kernel_spmd(nc, in_maps, core_ids=list(range(N_CORES)))
    outs = [np.asarray(res.results[c]["out"]).reshape(SEQ_PER_CORE, C, D)
            for c in range(N_CORES)]
    return np.concatenate(outs, axis=0)


# version 05 (final)
# speedup vs baseline: 1.0065x; 1.0065x over previous
import math
import os
import numpy as np
import ml_dtypes
from contextlib import ExitStack

import concourse.bass as bass
import concourse.mybir as mybir
from concourse.bass_utils import run_bass_kernel_spmd

F32 = mybir.dt.float32
BF16 = mybir.dt.bfloat16
AF = mybir.ActivationFunctionType
ALU = mybir.AluOpType
AX = mybir.AxisListType

N_CORES = 8
B = 16
C = 2048
D = 2048
NH = 16
HD = 128
HD2 = 64
NC = D // 128
NTT = C // 128
SEQ_PER_CORE = B // N_CORES
N_EXP = 32
F_EXP = 768
F_SH = 1536
CHUNK = 1024
EPS = 1e-5
S128 = 1.0 / math.sqrt(128.0)
MROT = 128.0 ** (-0.25)
FT_W = 256
NPQ = D // FT_W

ENGINES = ("tensor", "vector", "scalar", "gpsimd", "sync")
_SEM_ENGINE = {}


class Tok:
    __slots__ = ("sem", "val")
    def __init__(self, sem, val):
        self.sem = sem
        self.val = val


class Sched:
    def __init__(self, nc, es):
        self.nc = nc
        self.ops = {e: [] for e in ENGINES}
        self.cnt = {}
        self.sems = {}
        self.hwm = {e: {} for e in ENGINES}
        self.es = es
        self.free = {}
        self.ring = 0
        self.nops = 0
        self.limit = int(os.environ.get("K_MAXOPS", "0"))

    def sem(self, name):
        if name not in self.sems:
            self.sems[name] = self.es.enter_context(self.nc.semaphore(name))
            self.cnt[name] = 0
        return self.sems[name]

    def dep(self, key):
        return self.free.get(key)

    def setf(self, key, tok):
        self.free[key] = tok

    def emit(self, eng, fn, deps=(), inc=None, inc_by=1):
        self.nops += 1
        if self.limit and self.nops > self.limit:
            tok = None
            if inc is not None:
                self.sem(inc)
                self.cnt[inc] += inc_by
                tok = Tok(inc, self.cnt[inc])
                sem_h = self.sems[inc]
                self.ops[eng].append(
                    lambda e, s_=sem_h, k_=inc_by: e.sem_inc(s_, k_))
            return tok
        tok = None
        if inc is not None:
            self.sem(inc)
            self.cnt[inc] += inc_by
            tok = Tok(inc, self.cnt[inc])
        byname = {}
        own = _SEM_ENGINE.get(eng, ())
        for d in deps:
            if d is None:
                continue
            if d.sem in own:
                continue
            if byname.get(d.sem, -1) < d.val:
                byname[d.sem] = d.val
        waits = []
        hw = self.hwm[eng]
        for nm, val in byname.items():
            if hw.get(nm, -1) >= val:
                continue
            hw[nm] = val
            waits.append((self.sems[nm], val))
        inc_sem = self.sems[inc] if inc is not None else None

        def closure(e, waits=waits, fn=fn, inc_sem=inc_sem, inc_by=inc_by):
            for s, v in waits:
                e.wait_ge(s, v)
            ins = fn(e)
            if inc_sem is not None:
                ins.then_inc(inc_sem, inc_by)
        self.ops[eng].append(closure)
        return tok

    def wait_only(self, eng, toks):
        self.emit(eng, lambda e: None, deps=toks)

    def final_toks(self):
        return [Tok(s, self.cnt[s]) for s in self.sems if self.cnt[s] > 0]

    def replay(self, block):
        for eng in ENGINES:
            ops = self.ops[eng]
            if not ops:
                continue
            dec = getattr(block, eng)

            def section(e, ops=ops):
                for f in ops:
                    f(e)
            dec(section)
        self.ops = {e: [] for e in ENGINES}


class TSlots:
    def __init__(self, pst):
        self.pst = pst
        self.i = 0
    def get(self):
        b = self.i % 2
        s2 = (self.i // 2) % 8
        self.i += 1
        return (self.pst[:, b, s2 * 128:(s2 + 1) * 128], ("pst", b, s2), ("pstB", b))


class VecSlots:
    def __init__(self, t, n):
        self.t = t
        self.n = n
        self.i = 0
    def get(self):
        i = self.i
        self.i = (self.i + 1) % self.n
        return self.t[:, i:i + 1]


def _vec_pc(v):
    return np.ascontiguousarray(np.asarray(v).reshape(-1, 128).T)


def prep_host(inputs):
    ii = inputs
    sh = {}
    def pm(w, d_in, blk):
        w = np.asarray(w)
        f = w.shape[1]
        return np.ascontiguousarray(
            w.reshape(d_in // 128, 128, f // blk, blk).transpose(2, 1, 0, 3))
    sh["WqS"] = pm(ii["L0_attn_Wq"], D, FT_W)
    sh["WkvS"] = pm(ii["L0_attn_Wkv"], D, FT_W)
    sh["WoS"] = np.ascontiguousarray(
        np.asarray(ii["L0_attn_Wo"]).reshape(NH, 128, 4, 512).transpose(0, 2, 1, 3))
    sh["gwS"] = np.ascontiguousarray(
        np.asarray(ii["L0_ffn_gate_w"]).reshape(NC, 128, N_EXP).transpose(1, 0, 2))
    sh["WgS"] = np.ascontiguousarray(
        np.asarray(ii["L0_ffn_wg"]).reshape(N_EXP, NC, 128, F_EXP).transpose(0, 2, 1, 3))
    sh["WuS"] = np.ascontiguousarray(
        np.asarray(ii["L0_ffn_wu"]).reshape(N_EXP, NC, 128, F_EXP).transpose(0, 2, 1, 3))
    sh["WdS"] = np.ascontiguousarray(
        np.asarray(ii["L0_ffn_wd"]).reshape(N_EXP, 6, 128, D).transpose(0, 2, 1, 3))
    sh["SgS"] = np.ascontiguousarray(
        np.asarray(ii["L0_ffn_sg"]).reshape(NC, 128, 2, 768).transpose(2, 1, 0, 3))
    sh["SuS"] = np.ascontiguousarray(
        np.asarray(ii["L0_ffn_su"]).reshape(NC, 128, 2, 768).transpose(2, 1, 0, 3))
    sh["SdS"] = np.ascontiguousarray(
        np.asarray(ii["L0_ffn_sd"]).reshape(12, 128, 2, 1024).transpose(2, 1, 0, 3))
    sh["gq"] = _vec_pc(ii["L0_attn_ln_q_g"]); sh["bqpre"] = _vec_pc(ii["L0_attn_ln_q_b"])
    sh["gkv"] = _vec_pc(ii["L0_attn_ln_kv_g"]); sh["bkvpre"] = _vec_pc(ii["L0_attn_ln_kv_b"])
    sh["bq_row"] = np.ascontiguousarray(np.asarray(ii["L0_attn_bq"]).reshape(1, D))
    sh["bkv_row"] = np.ascontiguousarray(np.asarray(ii["L0_attn_bkv"]).reshape(1, 2 * D))
    sh["bo_row"] = np.ascontiguousarray(np.asarray(ii["L0_attn_bo"]).reshape(1, D))
    sh["lnh_g"] = np.ascontiguousarray(np.asarray(ii["L0_attn_ln_h_g"]).reshape(128, 1))
    sh["lnh_b"] = np.ascontiguousarray(np.asarray(ii["L0_attn_ln_h_b"]).reshape(128, 1))
    sh["c_eye"] = np.eye(128, dtype=np.float32).astype(ml_dtypes.bfloat16)
    sh["c_eye_f"] = np.eye(128, dtype=np.float32)
    sh["c_ones_row"] = np.ones((1, 128), dtype=ml_dtypes.bfloat16)
    sh["c_ones_col"] = np.ones((128, 1), dtype=ml_dtypes.bfloat16)
    f = (np.float32(8000.0) / np.float32(200.0)) ** np.linspace(
        0.0, 1.0, HD2, dtype=np.float32) * np.float32(200.0) / np.float32(1000.0)
    freqs = (np.float32(200.0) * f / np.float32(1000.0)).astype(np.float32)
    t = np.arange(C, dtype=np.float32)
    ang = np.einsum("i,j->ij", t, freqs).astype(np.float32)
    sh["c_cos"] = np.ascontiguousarray(
        np.cos(ang).astype(np.float32).reshape(NTT, 128, HD2).transpose(1, 0, 2))
    sh["c_sin"] = np.ascontiguousarray(
        np.sin(ang).astype(np.float32).reshape(NTT, 128, HD2).transpose(1, 0, 2))
    xs = [np.ascontiguousarray(np.asarray(inputs["L0_attn_x"])[2 * c:2 * c + 2])
          for c in range(N_CORES)]
    return sh, xs


def emit_ln_stats(sch, sb, vs, x_ap, W, deps, double_ln, mrot_slot=None):
    lvl = int(os.environ.get("K_STLEVEL", "99"))
    if lvl <= 0:
        t0 = sch.emit("vector", lambda e: e.memset(sb["vec"][:, 0:1], 1.0),
                      deps=list(deps), inc="v")
        return sb["cbias"][:, 0:1], sb["cbias"][:, 1:2], [t0]
    """LN stats over a [128, W] fp32 sbuf tile -> (rc, bc, toks): y = x*rc + bc.
    double_ln: combined LN(LN(x)) normalization (inner mean is exactly 0);
    optionally m' = sqrt(sum(LN(x)^2)) * MROT."""
    ssq = vs.get(); ssum = vs.get()
    if os.environ.get("K_NOSQ"):
        t_sq0 = sch.emit("vector", lambda e: e.tensor_mul(
            sb["dummy_f"][:, 0:W], x_ap, x_ap),
            deps=list(deps) + [sch.dep("dummy")], inc="v")
        t_sq = sch.emit("vector", lambda e: e.reduce_sum(
            ssq, sb["dummy_f"][:, 0:W], axis=AX.X), deps=[t_sq0], inc="v")
    else:
        t_sq = sch.emit("scalar", lambda e: e.activation(
            out=sb["dummy_bf"][:, 0:W], in_=x_ap, func=AF.Square, accum_out=ssq),
            deps=list(deps) + [sch.dep("dummy")], inc="s")
    t_sm = sch.emit("vector", lambda e: e.reduce_sum(ssum, x_ap, axis=AX.X),
                    deps=deps, inc="v")
    if lvl <= 1:
        return sb["cbias"][:, 0:1], sb["cbias"][:, 1:2], [t_sq, t_sm]
    inv = 1.0 / W
    mu = vs.get(); e2 = vs.get(); tmp = vs.get()
    ta = sch.emit("vector", lambda e, mu_=mu, ssum_=ssum: e.tensor_scalar(
        out=mu_, in0=ssum_, scalar1=inv, scalar2=None, op0=ALU.mult), deps=[t_sm], inc="v")
    tb = sch.emit("vector", lambda e, t_=tmp, ssum_=ssum: e.scalar_tensor_tensor(
        out=t_, in0=ssum_, scalar=inv, in1=ssum_, op0=ALU.mult, op1=ALU.mult),
        deps=[t_sm], inc="v")
    tc = sch.emit("vector", lambda e, e2_=e2, ssq_=ssq, t_=tmp: e.tensor_sub(e2_, ssq_, t_),
                  deps=[t_sq, tb], inc="v")
    if lvl <= 2:
        return sb["cbias"][:, 0:1], sb["cbias"][:, 1:2], [t_sq, t_sm, ta, tb, tc]
    lntmp = vs.get(); r1 = vs.get()
    ts1 = sch.emit("scalar", lambda e, a=lntmp, b=e2: e.activation(
        out=a, in_=b, func=AF.Ln, scale=inv, bias=sb["cbias"][:, 0:1]), deps=[tc], inc="s")
    ts2 = sch.emit("scalar", lambda e, a=r1, b=lntmp: e.activation(
        out=a, in_=b, func=AF.Exp, scale=-0.5), deps=[ts1], inc="s")
    toks = [t_sq, t_sm, ta, tb, tc, ts1, ts2]
    if lvl <= 3:
        return sb["cbias"][:, 0:1], sb["cbias"][:, 1:2], toks
    if not double_ln:
        bc = vs.get()
        tbc = sch.emit("vector", lambda e, a=bc, b=mu, c=r1: e.scalar_tensor_tensor(
            out=a, in0=b, scalar=-1.0, in1=c, op0=ALU.mult, op1=ALU.mult),
            deps=[ts2], inc="v")
        return r1, bc, toks + [tbc]
    r1sq = vs.get(); s2 = vs.get()
    td = sch.emit("vector", lambda e, a=r1sq, b=r1: e.scalar_tensor_tensor(
        out=a, in0=b, scalar=1.0, in1=b, op0=ALU.mult, op1=ALU.mult), deps=[ts2], inc="v")
    te = sch.emit("vector", lambda e, a=s2, b=e2, c=r1sq: e.tensor_mul(a, b, c),
                  deps=[td, tc], inc="v")
    lnt2 = vs.get(); r2 = vs.get(); rc = vs.get(); bc = vs.get()
    tf = sch.emit("scalar", lambda e, a=lnt2, b=s2: e.activation(
        out=a, in_=b, func=AF.Ln, scale=inv, bias=sb["cbias"][:, 0:1]), deps=[te], inc="s")
    tg = sch.emit("scalar", lambda e, a=r2, b=lnt2: e.activation(
        out=a, in_=b, func=AF.Exp, scale=-0.5), deps=[tf], inc="s")
    th = sch.emit("vector", lambda e, a=rc, b=r1, c=r2: e.tensor_mul(a, b, c),
                  deps=[tg], inc="v")
    ti = sch.emit("vector", lambda e, a=bc, b=mu, c=rc: e.scalar_tensor_tensor(
        out=a, in0=b, scalar=-1.0, in1=c, op0=ALU.mult, op1=ALU.mult), deps=[th], inc="v")
    toks += [td, te, tf, tg, th, ti]
    if mrot_slot is not None:
        lnm = vs.get()
        tj = sch.emit("scalar", lambda e, a=lnm, b=s2: e.activation(
            out=a, in_=b, func=AF.Ln), deps=[te], inc="s")
        tk = sch.emit("scalar", lambda e, a=mrot_slot, b=lnm: e.activation(
            out=a, in_=b, func=AF.Exp, scale=0.5, bias=sb["cbias"][:, 1:2]),
            deps=[tj], inc="s")
        toks += [tj, tk]
    return rc, bc, toks


def emit_attention(nc, sch, sb, ap, es, persist_tok, qn_scr, kn_scr, v_scr,
                   xmid_scr, n_seq, n_qg=4, n_ft=24, n_tt=NTT):
    vs = VecSlots(sb["vec"], 64)
    at = {}
    def sbt(name, shape, dt=F32):
        at[name] = es.enter_context(nc.sbuf_tensor("a_" + name, shape, dt))
        return at[name]
    sbt("xnfm", [128, NC, C], BF16)
    sbt("wsl", [128, 2, NC, FT_W], BF16)
    sbt("stage", [128, 2, 4096], BF16)
    sbt("attT", [128, 3, 512], BF16)
    sbt("afm", [128, NH, 512], BF16)
    sbt("qnb", [128, 2, 512], BF16)
    sbt("woring", [128, 3, 512], BF16)
    sbt("rre", [128, NTT, 128], BF16)
    sbt("rse", [128, NTT, 128], BF16)
    sbt("o_bf", [128, 4, D], BF16)
    sbt("xf", [128, 2, D], F32)
    sbt("of", [128, D], F32)
    sbt("xnt", [128, D], BF16)
    sbt("qrot", [128, FT_W], F32)
    sbt("qsw", [128, FT_W], F32)
    sbt("qn_tm", [128, 3, FT_W], BF16)
    sbt("brow", [1, 2, FT_W], BF16)
    sbt("rowbuf", [1, 2, FT_W])
    sbt("sumb", [1, 2, 512], BF16)
    sbt("recip", [128, 512], F32)
    sbt("hvec", [128, 16])
    sbt("cosT", [128, NTT, HD2]); sbt("sinT", [128, NTT, HD2])

    ps = es.enter_context(nc.psum_tensor("psA", [128, 6, 512], F32))
    pst = es.enter_context(nc.psum_tensor("psT", [128, 2, 1024], BF16))
    tsl = TSlots(pst)
    eye = sb["eye"]
    dep, setf = sch.dep, sch.setf
    if os.environ.get("K_NOSQ"):
        sb["dummy_f"] = at["of"]
    t_ct1 = sch.emit("sync", lambda e: e.dma_start(out=at["cosT"][:], in_=ap["c_cos"]),
                     inc="sct", inc_by=16)
    t_ct2 = sch.emit("sync", lambda e: e.dma_start(out=at["sinT"][:], in_=ap["c_sin"]),
                     inc="sct", inc_by=16)
    tab_tok = [Tok("sct", 32)]

    for s in range(n_seq):
        def p0_a(tt):
            sl = tt % 2
            xv = at["stage"][:, sl, 0:2048]
            xb = at["xf"][:, tt % 2, :]
            t_x = sch.emit("sync", lambda e, d=xb, s_=s, tt_=tt: e.dma_start(
                out=d, in_=ap["x"][s_, tt_ * 128:(tt_ + 1) * 128, :]),
                deps=[dep(("xf", tt % 2))] + (persist_tok if (s, tt) == (0, 0) else []),
                inc="sx%d" % (tt % 2), inc_by=16)
            mslot = sb["mrot"][:, tt:tt + 1]
            rc, bc, stoks = emit_ln_stats(sch, sb, vs, xb, D, [t_x], True, mslot)
            mtok = stoks[-1]
            cos_t, sin_t = at["cosT"][:, tt, :], at["sinT"][:, tt, :]
            rre_t, rse_t = at["rre"][:, tt, :], at["rse"][:, tt, :]
            sch.emit("vector", lambda e, o=rre_t, i=cos_t, m=mslot: e.tensor_scalar(
                out=o[:, 0:128:2], in0=i, scalar1=m, scalar2=None, op0=ALU.mult),
                deps=[mtok] + tab_tok, inc=None)
            sch.emit("vector", lambda e, o=rre_t, i=cos_t, m=mslot: e.tensor_scalar(
                out=o[:, 1:128:2], in0=i, scalar1=m, scalar2=None, op0=ALU.mult),
                inc=None)
            sch.emit("vector", lambda e, o=rse_t, i=sin_t, m=mslot: e.tensor_scalar(
                out=o[:, 0:128:2], in0=i, scalar1=m, scalar2=-1.0, op0=ALU.mult,
                op1=ALU.mult), inc=None)
            t_rt = sch.emit("vector", lambda e, o=rse_t, i=sin_t, m=mslot: e.tensor_scalar(
                out=o[:, 1:128:2], in0=i, scalar1=m, scalar2=None, op0=ALU.mult),
                inc="v")
            setf(("rot", tt), t_rt)
            t_xn = sch.emit("vector", lambda e, xb_=xb, rc_=rc, bc_=bc, xv_=xv:
                            e.tensor_scalar(
                out=xv_, in0=xb_, scalar1=rc_, scalar2=bc_,
                op0=ALU.mult, op1=ALU.add), deps=stoks + [dep(("stage", sl))], inc="v")
            setf(("xf", tt % 2), t_xn)
            return (tt, sl, t_xn, xv)

        def p0_b(item):
            tt, sl, t_xn, xv = item
            last_ev = None
            for c in range(NC):
                pap, pkey, bkey = tsl.get()
                t_tr = sch.emit("tensor", lambda e, c_=c, p_=pap, xv_=xv: e.transpose(
                    p_, xv_[:, c_ * 128:(c_ + 1) * 128],
                    eye[:]), deps=[t_xn, dep(pkey), dep(bkey), dep("xnfm_war")], inc="tr")
                if c % 2 == 0:
                    t_ev = sch.emit("scalar", lambda e, c_=c, tt_=tt, p_=pap: e.activation(
                        out=at["xnfm"][:, c_, tt_ * 128:(tt_ + 1) * 128],
                        in_=p_, func=AF.Copy),
                        deps=[t_tr], inc="s")
                else:
                    t_ev = sch.emit("vector", lambda e, c_=c, tt_=tt, p_=pap: e.tensor_copy(
                        at["xnfm"][:, c_, tt_ * 128:(tt_ + 1) * 128], p_),
                        deps=[t_tr], inc="v")
                setf(pkey, t_ev)
                setf(bkey, t_ev)
                last_ev = t_ev
            setf(("stage", sl), Tok("tr", sch.cnt["tr"]))
            setf(("xn_tt", tt), last_ev)

        p_prev = p0_a(0)
        for tt in range(1, n_tt):
            p_cur = p0_a(tt)
            p0_b(p_prev)
            p_prev = p_cur
        p0_b(p_prev)

        ftl = ([("q", i, i) for i in range(8)]
               + [("k", i, i) for i in range(8)]
               + [("v", i, i + 8) for i in range(8)])
        last_proj_mm = None
        pend = []

        def flush_tp(item):
            tt_, qsl_, t_nm_, path_, stb_ = item
            for hl in range(2):
                pap, pkey, bkey = tsl.get()
                t_tr = sch.emit("tensor", lambda e, hl_=hl, p_=pap, q_=qsl_: e.transpose(
                    p_, at["qn_tm"][:, q_, hl_ * HD:(hl_ + 1) * HD], eye[:]),
                    deps=[t_nm_[hl], dep(pkey), dep(bkey)], inc="tr")
                if path_ == "q":
                    sc_ap, bi_ap = sb["lnh"][:, 2:3], sb["lnh"][:, 3:4]
                else:
                    sc_ap, bi_ap = sb["lnh"][:, 0:1], sb["lnh"][:, 1:2]
                t_ev = sch.emit("scalar", lambda e, hl_=hl, tt_=tt_, p_=pap,
                                b_=stb_, sc_=sc_ap, bi_=bi_ap: e.activation(
                    out=at["stage"][:, b_, hl_ * 2048 + tt_ * 128:
                                    hl_ * 2048 + tt_ * 128 + 128],
                    in_=p_, func=AF.Identity,
                    scale=sc_, bias=bi_),
                    deps=[t_tr, dep(("stage", stb_))], inc="s")
                setf(pkey, t_ev)
                setf(bkey, t_ev)
                setf(("stg_last", stb_), t_ev)
            setf(("qn_tm", qsl_), Tok("tr", sch.cnt["tr"]))

        for idx, (path, fti, wsrc_i) in enumerate(ftl[:n_ft]):
            wb = idx % 2
            stb = idx % 2
            if path == "q":
                wsrc = ap["WqS"][wsrc_i, :, :, :]
                gv, bv_bf = sb["gq"], sb["bqpre_bf"]
                brow_hbm = ap["bq_row"][:, fti * FT_W:(fti + 1) * FT_W]
            else:
                wsrc = ap["WkvS"][wsrc_i, :, :, :]
                gv, bv_bf = sb["gkv"], sb["bkvpre_bf"]
                brow_hbm = ap["bkv_row"][:, wsrc_i * FT_W:(wsrc_i + 1) * FT_W]
            t_rowd = sch.emit("sync", lambda e, d=at["rowbuf"][0:1, wb, :], s_=brow_hbm:
                              e.dma_start(out=d, in_=s_),
                              deps=[dep(("rowbuf", wb))], inc="srb%d" % wb, inc_by=16)
            t_w = sch.emit("gpsimd", lambda e, d=at["wsl"][:, wb, :, :], s_=wsrc:
                           e.dma_start(out=d, in_=s_),
                           deps=[dep(("wsl", wb))] + (persist_tok if idx == 0 else []),
                           inc="gw%d" % wb, inc_by=16)
            t_bm = None
            for c in range(NC):
                t_bm = sch.emit("tensor", lambda e, c_=c, wb_=wb, bv_=bv_bf: e.matmul(
                    ps[0:1, 4, 0:FT_W], bv_[:, c_:c_ + 1], at["wsl"][:, wb_, c_, :],
                    start=(c_ == 0), stop=(c_ == NC - 1)),
                    deps=[t_w, dep(("psb4",))] if c == 0 else [],
                    inc="pe" if c == NC - 1 else None)
            t_br = sch.emit("vector", lambda e, wb_=wb: e.tensor_add(
                at["brow"][0:1, wb_, :], ps[0:1, 4, 0:FT_W], at["rowbuf"][0:1, wb_, :]),
                deps=[t_bm, t_rowd, dep(("brow", wb))], inc="v")
            setf(("psb4",), t_br)
            setf(("rowbuf", wb), t_br)
            t_fold = None
            for c in range(NC):
                t_fold = sch.emit("vector", lambda e, c_=c, wb_=wb, gv_=gv: e.tensor_scalar(
                    out=at["wsl"][:, wb_, c_, :], in0=at["wsl"][:, wb_, c_, :],
                    scalar1=gv_[:, c_:c_ + 1], scalar2=None, op0=ALU.mult),
                    deps=[t_bm] if c == 0 else [], inc="v" if c == NC - 1 else None)
            for tt in range(NTT):
                sl = tt % 4
                bkkey = ("psb", sl) if sl < 2 else ("psb_av", sl - 2)
                for c in range(NC):
                    sch.emit("tensor", lambda e, c_=c, tt_=tt, sl_=sl, wb_=wb: e.matmul(
                        ps[:, sl_, 0:FT_W], at["xnfm"][:, c_, tt_ * 128:(tt_ + 1) * 128],
                        at["wsl"][:, wb_, c_, :], start=(c_ == 0), stop=False),
                        deps=([t_fold, dep(bkkey), dep(("xn_tt", tt))]
                              if c == 0 else []), inc=None)
                t_mm = sch.emit("tensor", lambda e, sl_=sl, wb_=wb: e.matmul(
                    ps[:, sl_, 0:FT_W], sb["ones_row"][:], at["brow"][0:1, wb_, :],
                    start=False, stop=True), deps=[t_br], inc="pe")
                last_proj_mm = t_mm
                setf(("brow", wb), t_mm)
                if path == "v":
                    t_ev = sch.emit("vector", lambda e, tt_=tt, sl_=sl, f_=fti, b_=stb:
                                    e.tensor_copy(
                        at["stage"][:, b_, tt_ * FT_W:(tt_ + 1) * FT_W],
                        ps[:, sl_, 0:FT_W]),
                        deps=[t_mm, dep(("stage", stb))], inc="v")
                    setf(bkkey, t_ev)
                    setf(("stg_last", stb), t_ev)
                else:
                    psq = ps[:, sl, 0:FT_W]
                    t_sw = sch.emit("vector", lambda e, psq_=psq: e.tensor_copy(
                        at["qsw"][:].rearrange("p (a b) -> p a b", b=2),
                        psq_.rearrange("p (a b) -> p a b", b=2)[:, :, ::-1]),
                        deps=[t_mm, dep("qsw")], inc="v")
                    rre2 = at["rre"][:, tt, :].unsqueeze(1).broadcast_to([128, 2, 128])
                    rse2 = at["rse"][:, tt, :].unsqueeze(1).broadcast_to([128, 2, 128])
                    t_m1 = sch.emit("vector", lambda e, psq_=psq, r_=rre2: e.tensor_tensor(
                        out=at["qrot"][:].rearrange("p (h f) -> p h f", f=128),
                        in0=psq_.rearrange("p (h f) -> p h f", f=128), in1=r_,
                        op=ALU.mult),
                        deps=[t_mm, dep("qrot"), dep(("rot", tt))], inc="v")
                    t_m2 = sch.emit("vector", lambda e, r_=rse2: e.tensor_tensor(
                        out=at["qsw"][:].rearrange("p (h f) -> p h f", f=128),
                        in0=at["qsw"][:].rearrange("p (h f) -> p h f", f=128), in1=r_,
                        op=ALU.mult), deps=[t_sw], inc="v")
                    t_add = sch.emit("vector", lambda e: e.tensor_add(
                        at["qrot"][:], at["qrot"][:], at["qsw"][:]),
                        deps=[t_m1, t_m2], inc="v")
                    setf(bkkey, t_m1)
                    setf("qsw", t_add)
                    t_sq2 = sch.emit("scalar", lambda e: e.activation(
                        out=sb["dummy_bf"][:, 0:FT_W], in_=at["qrot"][:],
                        func=AF.Square), deps=[t_add, dep("dummy")], inc="s")
                    hs = at["hvec"]
                    t_rs = sch.emit("vector", lambda e: e.reduce_sum(
                        hs[:, 0:2],
                        at["qrot"][:].rearrange("p (h f) -> p h f", f=HD),
                        axis=AX.X), deps=[t_add], inc="v")
                    t_rq = sch.emit("vector", lambda e: e.reduce_sum(
                        hs[:, 2:4],
                        sb["dummy_bf"][:, 0:FT_W].rearrange("p (h f) -> p h f", f=HD),
                        axis=AX.X), deps=[t_sq2], inc="v")
                    setf("dummy", t_rq)
                    t_a = sch.emit("vector", lambda e: e.tensor_scalar(
                        out=hs[:, 4:6], in0=hs[:, 0:2], scalar1=1.0 / HD, scalar2=None,
                        op0=ALU.mult), deps=[t_rs], inc="v")
                    t_b = sch.emit("vector", lambda e: e.tensor_mul(
                        hs[:, 6:8], hs[:, 0:2], hs[:, 4:6]), deps=[t_a], inc="v")
                    t_c = sch.emit("vector", lambda e: e.tensor_sub(
                        hs[:, 8:10], hs[:, 2:4], hs[:, 6:8]), deps=[t_b, t_rq], inc="v")
                    t_d = sch.emit("scalar", lambda e: e.activation(
                        out=hs[:, 10:12], in_=hs[:, 8:10], func=AF.Ln, scale=1.0 / HD,
                        bias=sb["cbias"][:, 0:1]), deps=[t_c], inc="s")
                    t_e = sch.emit("scalar", lambda e: e.activation(
                        out=hs[:, 12:14], in_=hs[:, 10:12], func=AF.Exp, scale=-0.5),
                        deps=[t_d], inc="s")
                    qsl = tt % 3
                    t_nm = [None, None]
                    for hl in range(2):
                        t_nm[hl] = sch.emit("vector", lambda e, hl_=hl, q_=qsl: e.tensor_scalar(
                            out=at["qn_tm"][:, q_, hl_ * HD:(hl_ + 1) * HD],
                            in0=at["qrot"][:, hl_ * HD:(hl_ + 1) * HD],
                            scalar1=hs[:, 4 + hl_:5 + hl_], scalar2=hs[:, 12 + hl_:13 + hl_],
                            op0=ALU.subtract, op1=ALU.mult),
                            deps=[t_e, dep(("qn_tm", qsl))], inc="v")
                    setf("qrot", t_nm[1])
                    pend.append((tt, qsl, t_nm, path, stb))
                    if len(pend) > 2:
                        flush_tp(pend.pop(0))
            while pend:
                flush_tp(pend.pop(0))
            t_last = dep(("stg_last", stb))
            for hl in range(2):
                h = fti * 2 + hl
                if path == "q":
                    t_d1 = sch.emit("sync", lambda e, h_=h, hl_=hl, b_=stb: e.dma_start(
                        out=qn_scr[h_, :, :],
                        in_=at["stage"][:, b_, hl_ * 2048:(hl_ + 1) * 2048]),
                        deps=[t_last], inc="ss%d" % stb, inc_by=16)
                elif path == "k":
                    t_d1 = sch.emit("sync", lambda e, h_=h, hl_=hl, b_=stb: e.dma_start(
                        out=kn_scr[h_, :, :],
                        in_=at["stage"][:, b_, hl_ * 2048:(hl_ + 1) * 2048]),
                        deps=[t_last], inc="ss%d" % stb, inc_by=16)
                else:
                    t_d1 = sch.emit("sync", lambda e, h_=h, hl_=hl, b_=stb: e.dma_start(
                        out=v_scr[h_, :, :, :],
                        in_=at["stage"][:, b_, :].rearrange(
                            "p (t j) -> p t j", j=FT_W)[:, :, hl_ * 128:(hl_ + 1) * 128]),
                        deps=[t_last], inc="ss%d" % stb, inc_by=16)
                setf(("stage", stb), t_d1)
            setf(("wsl", wb), last_proj_mm)
        setf("xnfm_war", last_proj_mm)

        if n_ft < 24:
            n_qg = 0
        for qg in range(n_qg):
            t_af_prev = dep("afm")
            SCB = (0, 1, 5)
            fin_pend = []

            def finish_h(item):
                h_, t_cast, t_sum_, t_av_ = item
                bh = SCB[h_ % 3]
                avb = 2 + h_ % 2
                t_bc = sch.emit("tensor", lambda e, b_=bh, h2=h_ % 2: e.matmul(
                    ps[:, b_, :], sb["ones_row"][:], at["sumb"][0:1, h2, :],
                    start=True, stop=True),
                    deps=[t_cast, dep(("psb", bh))], inc="pe")
                t_rc = sch.emit("vector", lambda e, b_=bh: e.reciprocal(
                    at["recip"][:], ps[:, b_, :]),
                    deps=[t_bc, dep("recip")], inc="v")
                setf(("psb", bh), t_rc)
                t_ae = sch.emit("vector", lambda e, h2=h_, a_=avb: e.tensor_tensor(
                    out=at["afm"][:, h2, :], in0=ps[:, a_, :], in1=at["recip"][:],
                    op=ALU.mult), deps=[t_av_, t_rc, t_af_prev], inc="v")
                setf("recip", t_ae)
                setf(("psb_av", h_ % 2), t_ae)
                setf(("sumb", h_ % 2), t_bc)
                setf(("afm_w", h_), t_ae)

            def kn_v(b):
                return at["stage"][:, b, 0:2048]
            def vv_v(b):
                return at["stage"][:, b, 2048:4096]

            def emit_loads(h):
                b = h % 2
                spill_done = ([Tok("ss0", sch.cnt.get("ss0", 0)),
                               Tok("ss1", sch.cnt.get("ss1", 0))]
                              if (qg, h) == (0, 0) else [])
                t_kn = sch.emit("sync", lambda e, h_=h, b_=b: e.dma_start(
                    out=kn_v(b_), in_=kn_scr[h_, :, :]),
                    deps=[dep(("stage", b))] + spill_done, inc="skn%d" % b, inc_by=16)
                t_vv = sch.emit("sync", lambda e, h_=h, b_=b: e.dma_start(
                    out=vv_v(b_).rearrange("p (t j) -> p t j", j=128),
                    in_=v_scr[h_, :, :, :]),
                    deps=[], inc="svv%d" % b, inc_by=16)
                t_qn = sch.emit("sync", lambda e, h_=h, b_=b, qg_=qg: e.dma_start(
                    out=at["qnb"][:, b_, :], in_=qn_scr[h_, :, qg_ * 512:(qg_ + 1) * 512]),
                    deps=[dep(("qnb", b))], inc="sqn%d" % b, inc_by=16)
                hload[h] = (t_kn, t_vv, t_qn)

            hload = {}
            emit_loads(0)
            for h in range(NH):
                if h + 1 < NH:
                    emit_loads(h + 1)
                t_kn, t_vv, t_qn = hload[h]
                hb = h % 2
                avb = 2 + h % 2
                t_sum = None
                t_av = None
                sa_pend = []

                def flush_sa(kt_, t_ex_):
                    nonlocal t_sum, t_av
                    t_sum = sch.emit("tensor", lambda e, k_=kt_: e.matmul(
                        ps[0:1, 4, :], sb["ones_col"][:], at["attT"][:, k_ % 3, :],
                        start=(k_ == 0), stop=(k_ == NTT - 1)),
                        deps=[t_ex_] + ([dep(("psb4",))] if kt_ == 0 else []), inc="pe")
                    t_av = sch.emit("tensor", lambda e, k_=kt_, a_=avb, b_=hb: e.matmul(
                        ps[:, a_, :], vv_v(b_)[:, k_ * 128:(k_ + 1) * 128],
                        at["attT"][:, k_ % 3, :],
                        start=(k_ == 0), stop=(k_ == NTT - 1)),
                        deps=[t_vv] + ([dep(("psb_av", h % 2))] if kt_ == 0 else []),
                        inc="pe")
                    setf(("attT", kt_ % 3), t_av)

                for kt in range(NTT):
                    scb = SCB[kt % 3]
                    t_sc = sch.emit("tensor", lambda e, kt_=kt, b_=scb, hb_=hb: e.matmul(
                        ps[:, b_, :], kn_v(hb_)[:, kt_ * 128:(kt_ + 1) * 128],
                        at["qnb"][:, hb_, :], start=True, stop=True),
                        deps=[t_kn, t_qn, dep(("psb", scb))], inc="pe")
                    t_ex = sch.emit("scalar", lambda e, kt_=kt, b_=scb: e.activation(
                        out=at["attT"][:, kt_ % 3, :], in_=ps[:, b_, :], func=AF.Exp),
                        deps=[t_sc, dep(("attT", kt % 3))], inc="s")
                    setf(("psb", scb), t_ex)
                    sa_pend.append((kt, t_ex))
                    if len(sa_pend) > 2:
                        flush_sa(*sa_pend.pop(0))
                while sa_pend:
                    flush_sa(*sa_pend.pop(0))
                setf(("stage", hb), t_av)
                setf(("qnb", hb), t_sc)
                t_cast = sch.emit("vector", lambda e, h2=h % 2: e.tensor_copy(
                    at["sumb"][0:1, h2, :], ps[0:1, 4, :]),
                    deps=[t_sum, dep(("sumb", h % 2))], inc="v")
                setf(("psb4",), t_cast)
                fin_pend.append((h, t_cast, t_sum, t_av))
                if len(fin_pend) > 1:
                    finish_h(fin_pend.pop(0))
            while fin_pend:
                finish_h(fin_pend.pop(0))
            for dq in range(4):
                for h in range(NH):
                    wslot = (dq * NH + h) % 3
                    t_wo = sch.emit("gpsimd", lambda e, h_=h, dq_=dq, w_=wslot: e.dma_start(
                        out=at["woring"][:, w_, :], in_=ap["WoS"][h_, dq_, :, :]),
                        deps=[dep(("wo", wslot))], inc="gwo%d" % wslot, inc_by=16)
                    t_om = None
                    for qt in range(4):
                        bk = (("psb", qt) if qt < 2 else ("psb_av", qt - 2))
                        t_om = sch.emit("tensor", lambda e, h_=h, qt_=qt, w_=wslot: e.matmul(
                            ps[:, qt_, :], at["afm"][:, h_, qt_ * 128:(qt_ + 1) * 128],
                            at["woring"][:, w_, :], start=(h_ == 0), stop=False),
                            deps=[t_wo, dep(("afm_w", h))]
                                 + ([dep(bk)] if h == 0 else []),
                            inc="pe" if qt == 3 else None)
                    setf(("wo", wslot), t_om)
                for qt in range(4):
                    t_ob = sch.emit("tensor", lambda e, qt_=qt, dq_=dq: e.matmul(
                        ps[:, qt_, :], sb["ones_row"][:],
                        sb["bo_row_bf"][0:1, dq_ * 512:(dq_ + 1) * 512],
                        start=False, stop=True), deps=[], inc="pe")
                    t_oe = sch.emit("scalar", lambda e, qt_=qt, dq_=dq: e.activation(
                        out=at["o_bf"][:, qt_, dq_ * 512:(dq_ + 1) * 512],
                        in_=ps[:, qt_, :], func=AF.Copy),
                        deps=[t_ob, dep(("o_bf", qt))], inc="s")
                    setf(("psb", qt) if qt < 2 else ("psb_av", qt - 2), t_oe)
                    t_last_o_ev = t_oe
                    setf(("o_ev", qt, dq), t_oe)
            setf("afm", Tok("pe", sch.cnt["pe"]))
            for qt in range(4):
                evs = [sch.dep(("o_ev", qt, dq)) for dq in range(4)]
                ob = at["o_bf"][:, qt, :]
                r0 = s * C + qg * 512 + qt * 128
                t_x2 = sch.emit("sync", lambda e, r_=qg * 512 + qt * 128, s_=s, q_=qt % 2:
                                e.dma_start(
                    out=at["xf"][:, q_, :], in_=ap["x"][s_, r_:r_ + 128, :]),
                    deps=[dep(("xf", qt % 2))], inc="sx%d" % (qt % 2), inc_by=16)
                ssq = vs.get(); ssum = vs.get()
                t_sq = sch.emit("scalar", lambda e, o_=ob, a=ssq: e.activation(
                    out=sb["dummy_bf"][:], in_=o_, func=AF.Square, accum_out=a),
                    deps=evs + [dep("dummy")], inc="s")
                t_sm = sch.emit("vector", lambda e, o_=ob, a=ssum: e.reduce_sum(
                    a, o_, axis=AX.X), deps=evs, inc="v")
                setf("dummy", t_sq)
                mu = vs.get(); e2 = vs.get(); tmp = vs.get(); lnt = vs.get(); r1 = vs.get()
                ta = sch.emit("vector", lambda e, a=mu, b=ssum: e.tensor_scalar(
                    out=a, in0=b, scalar1=1.0 / D, scalar2=None, op0=ALU.mult),
                    deps=[t_sm], inc="v")
                tb = sch.emit("vector", lambda e, a=tmp, b=ssum: e.scalar_tensor_tensor(
                    out=a, in0=b, scalar=1.0 / D, in1=b, op0=ALU.mult, op1=ALU.mult),
                    deps=[t_sm], inc="v")
                tc = sch.emit("vector", lambda e, a=e2, b=ssq, c=tmp: e.tensor_sub(a, b, c),
                              deps=[t_sq, tb], inc="v")
                td = sch.emit("scalar", lambda e, a=lnt, b=e2: e.activation(
                    out=a, in_=b, func=AF.Ln, scale=1.0 / D, bias=sb["cbias"][:, 0:1]), deps=[tc], inc="s")
                te = sch.emit("scalar", lambda e, a=r1, b=lnt: e.activation(
                    out=a, in_=b, func=AF.Exp, scale=-0.5), deps=[td], inc="s")
                tf = sch.emit("vector", lambda e, o_=ob, a=mu, b=r1: e.tensor_scalar(
                    out=at["of"][:], in0=o_, scalar1=a, scalar2=b,
                    op0=ALU.subtract, op1=ALU.mult), deps=[te, dep("of")], inc="v")
                tg = sch.emit("vector", lambda e, qt_=qt: e.tensor_add(
                    at["of"][:], at["of"][:], at["xf"][:, qt_ % 2, :]),
                    deps=[tf, t_x2], inc="v")
                setf(("xf", qt % 2), tg)
                setf(("o_bf", qt), tg)
                t_st = sch.emit("sync", lambda e, r_=r0: e.dma_start(
                    out=xmid_scr[r_:r_ + 128, :], in_=at["of"][:]),
                    deps=[tg], inc="sxd", inc_by=16)
                setf("of", t_st)
    return Tok("sxd", sch.cnt["sxd"]) if n_qg else None


def emit_ffn(nc, sch, sb, ap, es, persist_tok, xmid_tok, xmid_scr, comb_scr, out_ap,
             n_chunk=4):
    vs = VecSlots(sb["vec"], 64)
    ft = {}
    def sbt(name, shape, dt=F32):
        ft[name] = es.enter_context(nc.sbuf_tensor("f_" + name, shape, dt))
        return ft[name]
    sbt("xn2", [128, NC, CHUNK], BF16)
    sbt("lofm", [128, NC, 128], BF16)
    sbt("yacc", [128, 8, D], F32)
    sbt("scr", [128, 2, 6, CHUNK], BF16)
    sbt("wring", [128, 4, 4096], BF16)
    sbt("gwhi", [128, NC, N_EXP], BF16)
    sbt("gwlo", [128, NC, N_EXP], BF16)
    sbt("gwf", [128, NC, N_EXP], F32)
    sbt("scores", [128, 8, N_EXP], F32)
    sbt("m8", [128, 8], F32)
    sbt("mask", [128, N_EXP], F32)
    sbt("combT", [32, 128], BF16)
    sbt("crow", [1, 2, CHUNK], BF16)
    sbt("cb", [128, CHUNK], BF16)
    sbt("xn2f", [128, D], F32)
    sbt("gs", [128, CHUNK], BF16)
    sbt("us", [128, CHUNK], BF16)
    sbt("xm", [128, D], F32)

    ps = es.enter_context(nc.psum_tensor("psF", [128, 6, 512], F32))
    pst = es.enter_context(nc.psum_tensor("psFT", [128, 2, 1024], BF16))
    tsl = TSlots(pst)
    eye = sb["eye"]
    dep, setf = sch.dep, sch.setf
    g_ps = ps[:, 0:2, :].rearrange("p a b -> p (a b)")
    u_ps = ps[:, 2:4, :].rearrange("p a b -> p (a b)")

    t_g1 = sch.emit("gpsimd", lambda e: e.dma_start(out=ft["gwhi"][:], in_=ap["gwS"]),
                    deps=[xmid_tok], inc="gm1", inc_by=16)
    t_g2 = sch.emit("sync", lambda e: e.dma_start(out=ft["gwf"][:], in_=ap["gwS"]),
                    deps=[xmid_tok], inc="gm2", inc_by=16)
    t_g3 = sch.emit("vector", lambda e: e.tensor_sub(ft["gwlo"][:], ft["gwf"][:], ft["gwhi"][:]),
                    deps=[t_g1, t_g2], inc="v")

    for k in range(n_chunk):
        r0 = k * CHUNK
        def f0_a(tt):
            sl = tt % 2
            hi_v = ft["scr"][:, 0, sl * 2:sl * 2 + 2, :].rearrange("p a b -> p (a b)")
            lo_v = ft["scr"][:, 1, sl * 2:sl * 2 + 2, :].rearrange("p a b -> p (a b)")
            xb = ft["xm"][:]
            t_x = sch.emit("sync", lambda e, d=xb, r_=r0 + tt * 128: e.dma_start(
                out=d, in_=xmid_scr[r_:r_ + 128, :]),
                deps=[dep("fxm"), xmid_tok], inc="sxm", inc_by=16)
            rc, bc, stoks = emit_ln_stats(sch, sb, vs, xb, D, [t_x], False)
            t_xf = sch.emit("vector", lambda e, xb_=xb, rc_=rc, bc_=bc: e.tensor_scalar(
                out=ft["xn2f"][:], in0=xb_, scalar1=rc_, scalar2=bc_,
                op0=ALU.mult, op1=ALU.add), deps=stoks + [dep("xn2f")], inc="v")
            setf("fxm", t_xf)
            t_hi = sch.emit("vector", lambda e, h_=hi_v: e.tensor_copy(
                h_, ft["xn2f"][:]),
                deps=[t_xf, dep(("act", 0, sl * 2)), dep(("act", 0, sl * 2 + 1))],
                inc="v")
            t_lo = sch.emit("vector", lambda e, h_=hi_v, l_=lo_v: e.tensor_sub(
                l_, ft["xn2f"][:], h_),
                deps=[t_hi, dep(("act", 1, sl * 2)), dep(("act", 1, sl * 2 + 1))],
                inc="v")
            setf("xn2f", t_lo)
            return (tt, sl, t_hi, t_lo, hi_v, lo_v)

        def f0_b(item):
            tt, sl, t_hi, t_lo, hi_v, lo_v = item
            for c in range(NC):
                pap, pkey, bkey = tsl.get()
                t_tr = sch.emit("tensor", lambda e, c_=c, p_=pap, h_=hi_v: e.transpose(
                    p_, h_[:, c_ * 128:(c_ + 1) * 128],
                    eye[:]), deps=[t_hi, dep(pkey), dep(bkey), dep("fwar")], inc="tr")
                t_ev = sch.emit("scalar", lambda e, c_=c, tt_=tt, p_=pap: e.activation(
                    out=ft["xn2"][:, c_, tt_ * 128:(tt_ + 1) * 128],
                    in_=p_, func=AF.Copy),
                    deps=[t_tr], inc="s")
                setf(pkey, t_ev)
                setf(bkey, t_ev)
                pap2, pkey2, bkey2 = tsl.get()
                t_tr2 = sch.emit("tensor", lambda e, c_=c, p_=pap2, l_=lo_v: e.transpose(
                    p_, l_[:, c_ * 128:(c_ + 1) * 128],
                    eye[:]), deps=[t_lo, dep(pkey2), dep(bkey2)], inc="tr")
                t_ev2 = sch.emit("vector", lambda e, c_=c, p_=pap2: e.tensor_copy(
                    ft["lofm"][:, c_, :], p_), deps=[t_tr2, dep(("lofm", c))], inc="v")
                setf(pkey2, t_ev2)
                setf(bkey2, t_ev2)
                setf(("xn2_tt", tt), t_ev)
                setf(("lofm_ev", c), t_ev2)
            trtok = Tok("tr", sch.cnt["tr"])
            for jj in (sl * 2, sl * 2 + 1):
                setf(("act", 0, jj), trtok)
                setf(("act", 1, jj), trtok)
            t_gm = None
            for c in range(NC):
                t_gm = sch.emit("tensor", lambda e, c_=c, tt_=tt: e.matmul(
                    ps[:, 4, 0:N_EXP], ft["xn2"][:, c_, tt_ * 128:(tt_ + 1) * 128],
                    ft["gwhi"][:, c_, :], start=(c_ == 0), stop=False),
                    deps=([t_g1, dep(("ps", 4)), dep(("xn2_tt", tt))] if c == 0 else []),
                    inc=None)
            for c in range(NC):
                t_gm = sch.emit("tensor", lambda e, c_=c: e.matmul(
                    ps[:, 4, 0:N_EXP], ft["lofm"][:, c_, :], ft["gwhi"][:, c_, :],
                    start=False, stop=False), deps=[dep(("lofm_ev", c))], inc=None)
            for c in range(NC):
                t_gm = sch.emit("tensor", lambda e, c_=c, tt_=tt: e.matmul(
                    ps[:, 4, 0:N_EXP], ft["xn2"][:, c_, tt_ * 128:(tt_ + 1) * 128],
                    ft["gwlo"][:, c_, :], start=False, stop=(c_ == NC - 1)),
                    deps=[t_g3] if c == 0 else [], inc="pe" if c == NC - 1 else None)
            for c in range(NC):
                setf(("lofm", c), t_gm)
            t_s1 = sch.emit("scalar", lambda e, tt_=tt: e.activation(
                out=ft["mask"][:], in_=ps[:, 4, 0:N_EXP], func=AF.Exp, scale=-1.0),
                deps=[t_gm, dep("fmask")], inc="s")
            setf(("ps", 4), t_s1)
            t_s2 = sch.emit("vector", lambda e: e.tensor_scalar(
                out=ft["mask"][:], in0=ft["mask"][:], scalar1=1.0, scalar2=None,
                op0=ALU.add), deps=[t_s1], inc="v")
            t_s3 = sch.emit("vector", lambda e, tt_=tt: e.reciprocal(
                ft["scores"][:, tt_, :], ft["mask"][:]), deps=[t_s2], inc="v")
            t_m8 = sch.emit("vector", lambda e, tt_=tt: e.max(
                out=ft["m8"][:], in_=ft["scores"][:, tt_, :]), deps=[t_s3], inc="v")
            t_mk = sch.emit("vector", lambda e, tt_=tt: e.tensor_single_scalar(
                out=ft["mask"][:], in_=ft["scores"][:, tt_, :],
                scalar=ft["m8"][:, 5:6], op=ALU.is_ge), deps=[t_m8], inc="v")
            setf("fmask", t_mk)
            t_ms = sch.emit("vector", lambda e, tt_=tt: e.tensor_mul(
                ft["mask"][:], ft["mask"][:], ft["scores"][:, tt_, :]), deps=[t_mk], inc="v")
            ssum = vs.get(); rw = vs.get()
            t_sw = sch.emit("vector", lambda e, a=ssum: e.reduce_sum(
                a, ft["mask"][:], axis=AX.X), deps=[t_ms], inc="v")
            t_rw = sch.emit("vector", lambda e, a=ssum: e.tensor_scalar(
                out=a, in0=a, scalar1=1e-20, scalar2=None, op0=ALU.add), deps=[t_sw], inc="v")
            t_rw2 = sch.emit("vector", lambda e, a=ssum, b=rw: e.reciprocal(b, a),
                             deps=[t_rw], inc="v")
            t_cb = sch.emit("vector", lambda e, b=rw: e.tensor_scalar(
                out=ft["mask"][:], in0=ft["mask"][:], scalar1=b, scalar2=None,
                op0=ALU.mult), deps=[t_rw2], inc="v")
            t_ct = sch.emit("tensor", lambda e: e.transpose(
                ps[0:32, 4, 0:128], ft["mask"][:, 0:N_EXP], sb["eye_f"][:]),
                deps=[t_cb, dep(("ps", 4)), dep("combT_r")], inc="tr")
            t_ce = sch.emit("vector", lambda e: e.tensor_copy(
                ft["combT"][:], ps[0:32, 4, 0:128]), deps=[t_ct, dep("combT")], inc="v")
            setf(("ps", 4), t_ce)
            setf("fmask", t_ce)
            t_cst = sch.emit("sync", lambda e, tt_=tt, k_=k: e.dma_start(
                out=comb_scr[:, tt_ * 128:(tt_ + 1) * 128], in_=ft["combT"][:]),
                deps=[t_ce], inc="scmb", inc_by=16)
            setf("combT", t_cst)
            setf(("combw", tt), t_cst)

        prev_a = f0_a(0)
        for tt in range(1, 8):
            cur_a = f0_a(tt)
            f0_b(prev_a)
            prev_a = cur_a
        f0_b(prev_a)
        emit_ffn_shared(nc, sch, sb, ft, ap, ps, k)
        emit_ffn_experts(nc, sch, sb, ft, ap, ps, k, comb_scr)
        for tt in range(8):
            ya = ft["yacc"][:, tt, :]
            ydep = [dep(("yacc", tt))]
            xb = ft["xm"][:]
            t_x = sch.emit("sync", lambda e, d=xb, r_=r0 + tt * 128: e.dma_start(
                out=d, in_=xmid_scr[r_:r_ + 128, :]),
                deps=[dep("fxm")], inc="sxm", inc_by=16)
            ssq = vs.get(); ssum = vs.get()
            t_sq = sch.emit("scalar", lambda e, y_=ya, a=ssq: e.activation(
                out=sb["dummy_bf"][:], in_=y_, func=AF.Square, accum_out=a),
                deps=ydep + [dep("dummy")], inc="s")
            t_sm = sch.emit("vector", lambda e, y_=ya, a=ssum: e.reduce_sum(
                a, y_, axis=AX.X), deps=ydep, inc="v")
            setf("dummy", t_sq)
            mu = vs.get(); e2 = vs.get(); tmp = vs.get(); lnt = vs.get(); r1 = vs.get()
            ta = sch.emit("vector", lambda e, a=mu, b=ssum: e.tensor_scalar(
                out=a, in0=b, scalar1=1.0 / D, scalar2=None, op0=ALU.mult),
                deps=[t_sm], inc="v")
            tb = sch.emit("vector", lambda e, a=tmp, b=ssum: e.scalar_tensor_tensor(
                out=a, in0=b, scalar=1.0 / D, in1=b, op0=ALU.mult, op1=ALU.mult),
                deps=[t_sm], inc="v")
            tc = sch.emit("vector", lambda e, a=e2, b=ssq, c=tmp: e.tensor_sub(a, b, c),
                          deps=[t_sq, tb], inc="v")
            td = sch.emit("scalar", lambda e, a=lnt, b=e2: e.activation(
                out=a, in_=b, func=AF.Ln, scale=1.0 / D, bias=sb["cbias"][:, 0:1]), deps=[tc], inc="s")
            te = sch.emit("scalar", lambda e, a=r1, b=lnt: e.activation(
                out=a, in_=b, func=AF.Exp, scale=-0.5), deps=[td], inc="s")
            tf = sch.emit("vector", lambda e, y_=ya, a=mu, b=r1: e.tensor_scalar(
                out=ft["xn2f"][:], in0=y_, scalar1=a, scalar2=b,
                op0=ALU.subtract, op1=ALU.mult), deps=[te, dep("xn2f")], inc="v")
            tg = sch.emit("vector", lambda e, xb_=xb: e.tensor_add(
                ft["xn2f"][:], ft["xn2f"][:], xb_), deps=[tf, t_x], inc="v")
            setf("fxm", tg)
            setf(("yacc", tt), tg)
            t_st = sch.emit("sync", lambda e, r_=r0 + tt * 128: e.dma_start(
                out=out_ap[r_ // C, (r_ % C):(r_ % C) + 128, :], in_=ft["xn2f"][:]),
                deps=[tg], inc="sout", inc_by=16)
            setf("xn2f", t_st)
        setf("fwar", Tok("pe", sch.cnt["pe"]))


def emit_ffn_shared(nc, sch, sb, ft, ap, ps, k):
    dep, setf = sch.dep, sch.setf
    def ring_slot():
        s = sch.ring % 4
        sch.ring += 1
        return s
    for half in range(2):
        for third in range(3):
            sgs = ring_slot(); sus = ring_slot()
            t_wg = sch.emit("gpsimd", lambda e, h_=half, t_=third, s_=sgs: e.dma_start(
                out=ft["wring"][:, s_, :].rearrange("p (c f) -> p c f", f=FT_W),
                in_=ap["SgS"][h_, :, :, t_ * FT_W:(t_ + 1) * FT_W]),
                deps=[dep(("wr", sgs))], inc="gwr%d" % sgs, inc_by=16)
            t_wu = sch.emit("gpsimd", lambda e, h_=half, t_=third, s_=sus: e.dma_start(
                out=ft["wring"][:, s_, :].rearrange("p (c f) -> p c f", f=FT_W),
                in_=ap["SuS"][h_, :, :, t_ * FT_W:(t_ + 1) * FT_W]),
                deps=[dep(("wr", sus))], inc="gwr%d" % sus, inc_by=16)
            wg_v = ft["wring"][:, sgs, :].rearrange("p (c f) -> p c f", f=FT_W)
            wu_v = ft["wring"][:, sus, :].rearrange("p (c f) -> p c f", f=FT_W)
            t_gm = t_um = None
            for fs in range(2):
                lf = third * 2 + fs
                for n2 in range(2):
                    for c in range(NC):
                        t_gm = sch.emit("tensor", lambda e, c_=c, n_=n2, f_=fs, w_=wg_v:
                                        e.matmul(
                            ps[:, n_, :], w_[:, c_, f_ * 128:(f_ + 1) * 128],
                            ft["xn2"][:, c_, n_ * 512:(n_ + 1) * 512],
                            start=(c_ == 0), stop=(c_ == NC - 1)),
                            deps=([t_wg, dep(("ps", n2))]
                                  + [dep(("xn2_tt", t)) for t in range(n2 * 4, n2 * 4 + 4)]
                                  if c == 0 else []),
                            inc="pe" if c == NC - 1 else None)
                    t_sil = sch.emit("scalar", lambda e, n_=n2: e.activation(
                        out=ft["gs"][:, n_ * 512:(n_ + 1) * 512], in_=ps[:, n_, :],
                        func=AF.Silu), deps=[t_gm, dep(("gs", n2))], inc="s")
                    setf(("ps", n2), t_sil)
                    for c in range(NC):
                        t_um = sch.emit("tensor", lambda e, c_=c, n_=n2, f_=fs, w_=wu_v:
                                        e.matmul(
                            ps[:, 2 + n_, :], w_[:, c_, f_ * 128:(f_ + 1) * 128],
                            ft["xn2"][:, c_, n_ * 512:(n_ + 1) * 512],
                            start=(c_ == 0), stop=(c_ == NC - 1)),
                            deps=([t_wu, dep(("ps", 2 + n2))] if c == 0 else []),
                            inc="pe" if c == NC - 1 else None)
                    t_ue = sch.emit("vector", lambda e, n_=n2: e.tensor_copy(
                        ft["us"][:, n_ * 512:(n_ + 1) * 512], ps[:, 2 + n_, :]),
                        deps=[t_um, dep(("us", n2))], inc="v")
                    setf(("ps", 2 + n2), t_ue)
                    t_ss = sch.emit("vector", lambda e, h_=half, l_=lf, n_=n2: e.tensor_mul(
                        ft["scr"][:, h_, l_, n_ * 512:(n_ + 1) * 512],
                        ft["gs"][:, n_ * 512:(n_ + 1) * 512],
                        ft["us"][:, n_ * 512:(n_ + 1) * 512]),
                        deps=[t_sil, t_ue, dep(("act", half, lf))], inc="v")
                    setf(("gs", n2), t_ss)
                    setf(("us", n2), t_ss)
                    setf(("ss", half, lf), t_ss)
            setf(("wr", sgs), t_gm)
            setf(("wr", sus), t_um)
    ss_deps = [dep(("ss", hh, j)) for hh in range(2) for j in range(6)]
    for dh in range(2):
        pieces = []
        for p3 in range(3):
            sl = ring_slot()
            t_sd = sch.emit("gpsimd", lambda e, h_=dh, p_=p3, s_=sl: e.dma_start(
                out=ft["wring"][:, s_, :].rearrange("p (c f) -> p c f", f=1024),
                in_=ap["SdS"][h_, :, p_ * 4:(p_ + 1) * 4, :]),
                deps=[dep(("wr", sl))], inc="gwr%d" % sl, inc_by=16)
            pieces.append((sl, t_sd))
        for tt in range(8):
            for d2 in range(2):
                dq = dh * 2 + d2
                psl = 4 + (tt * 2 + d2) % 2
                t_dm = None
                for fcg in range(12):
                    sl, t_sd = pieces[fcg // 4]
                    sd_v = ft["wring"][:, sl, :].rearrange("p (c f) -> p c f", f=1024)
                    t_dm = sch.emit("tensor", lambda e, f_=fcg, tt_=tt, d_=d2,
                                    sl_=psl, w_=sd_v: e.matmul(
                        ps[:, sl_, :],
                        ft["scr"][:, f_ // 6, f_ % 6, tt_ * 128:(tt_ + 1) * 128],
                        w_[:, f_ % 4, d_ * 512:(d_ + 1) * 512],
                        start=(f_ == 0), stop=(f_ == 11)),
                        deps=([t_sd, dep(("ps", psl))] + ss_deps if fcg == 0 else [t_sd]),
                        inc="pe" if fcg == 11 else None)
                t_ye = sch.emit("vector", lambda e, tt_=tt, q_=dq, sl_=psl: e.tensor_copy(
                    ft["yacc"][:, tt_, q_ * 512:(q_ + 1) * 512], ps[:, sl_, :]),
                    deps=[t_dm, dep(("yacc", tt))], inc="v")
                setf(("ps", psl), t_ye)
                setf(("yaccw", tt, dq), t_ye)
        for sl, _ in pieces:
            setf(("wr", sl), Tok("pe", sch.cnt["pe"]))
    pe_now = Tok("pe", sch.cnt["pe"])
    for hh in range(2):
        for j in range(6):
            setf(("act", hh, j), pe_now)


def emit_ffn_experts(nc, sch, sb, ft, ap, ps, k, comb_scr):
    dep, setf = sch.dep, sch.setf
    def ring_slot():
        s = sch.ring % 4
        sch.ring += 1
        return s
    for e_i in range(N_EXP):
        ab = e_i % 2
        cw = [dep(("combw", t)) for t in range(8)] if e_i == 0 else []
        t_cr = sch.emit("sync", lambda e, e_=e_i, b_=ab: e.dma_start(
            out=ft["crow"][0:1, b_, :], in_=comb_scr[e_:e_ + 1, :]),
            deps=[dep(("crow", ab))] + cw, inc="scr%d" % ab, inc_by=16)
        crow = ft["crow"][0:1, ab, :]
        t_c1 = sch.emit("tensor", lambda e, r_=crow: e.matmul(
            ps[:, 5, :], sb["ones_row"][:], r_[:, 0:512], start=True, stop=True),
            deps=[dep(("ps", 5)), t_cr], inc="pe")
        t_c1e = sch.emit("vector", lambda e: e.tensor_copy(ft["cb"][:, 0:512], ps[:, 5, :]),
                         deps=[t_c1, dep("cb")], inc="v")
        t_c2 = sch.emit("tensor", lambda e, r_=crow: e.matmul(
            ps[:, 5, :], sb["ones_row"][:], r_[:, 512:1024], start=True, stop=True),
            deps=[t_c1e], inc="pe")
        t_c2e = sch.emit("vector", lambda e: e.tensor_copy(ft["cb"][:, 512:1024], ps[:, 5, :]),
                         deps=[t_c2], inc="v")
        setf(("ps", 5), t_c2e)
        setf(("crow", ab), Tok("pe", sch.cnt["pe"]))
        setf("combT_r", Tok("pe", sch.cnt["pe"]))
        cb_tok = t_c2e
        for third in range(3):
            sgs = ring_slot(); sus = ring_slot()
            t_wg = sch.emit("gpsimd", lambda e, e_=e_i, t_=third, s_=sgs: e.dma_start(
                out=ft["wring"][:, s_, :].rearrange("p (c f) -> p c f", f=FT_W),
                in_=ap["WgS"][e_, :, :, t_ * FT_W:(t_ + 1) * FT_W]),
                deps=[dep(("wr", sgs))], inc="gwr%d" % sgs, inc_by=16)
            t_wu = sch.emit("gpsimd", lambda e, e_=e_i, t_=third, s_=sus: e.dma_start(
                out=ft["wring"][:, s_, :].rearrange("p (c f) -> p c f", f=FT_W),
                in_=ap["WuS"][e_, :, :, t_ * FT_W:(t_ + 1) * FT_W]),
                deps=[dep(("wr", sus))], inc="gwr%d" % sus, inc_by=16)
            wg_v = ft["wring"][:, sgs, :].rearrange("p (c f) -> p c f", f=FT_W)
            wu_v = ft["wring"][:, sus, :].rearrange("p (c f) -> p c f", f=FT_W)
            t_gm = t_um = None
            for fs in range(2):
                lf = third * 2 + fs
                for n2 in range(2):
                    for c in range(NC):
                        t_gm = sch.emit("tensor", lambda e, c_=c, n_=n2, f_=fs, w_=wg_v:
                                        e.matmul(
                            ps[:, n_, :], w_[:, c_, f_ * 128:(f_ + 1) * 128],
                            ft["xn2"][:, c_, n_ * 512:(n_ + 1) * 512],
                            start=(c_ == 0), stop=(c_ == NC - 1)),
                            deps=([t_wg, dep(("ps", n2))] if c == 0 else []),
                            inc="pe" if c == NC - 1 else None)
                    t_sil = sch.emit("scalar", lambda e, n_=n2: e.activation(
                        out=ft["gs"][:, n_ * 512:(n_ + 1) * 512], in_=ps[:, n_, :],
                        func=AF.Silu), deps=[t_gm, dep(("gs", n2))], inc="s")
                    setf(("ps", n2), t_sil)
                    for c in range(NC):
                        t_um = sch.emit("tensor", lambda e, c_=c, n_=n2, f_=fs, w_=wu_v:
                                        e.matmul(
                            ps[:, 2 + n_, :], w_[:, c_, f_ * 128:(f_ + 1) * 128],
                            ft["xn2"][:, c_, n_ * 512:(n_ + 1) * 512],
                            start=(c_ == 0), stop=(c_ == NC - 1)),
                            deps=([t_wu, dep(("ps", 2 + n2))] if c == 0 else []),
                            inc="pe" if c == NC - 1 else None)
                    t_ue = sch.emit("vector", lambda e, n_=n2: e.tensor_tensor(
                        out=ft["us"][:, n_ * 512:(n_ + 1) * 512], in0=ps[:, 2 + n_, :],
                        in1=ft["cb"][:, n_ * 512:(n_ + 1) * 512], op=ALU.mult),
                        deps=[t_um, cb_tok, dep(("us", n2))], inc="v")
                    setf(("ps", 2 + n2), t_ue)
                    t_ac = sch.emit("vector", lambda e, a_=ab, l_=lf, n_=n2: e.tensor_mul(
                        ft["scr"][:, a_, l_, n_ * 512:(n_ + 1) * 512],
                        ft["gs"][:, n_ * 512:(n_ + 1) * 512],
                        ft["us"][:, n_ * 512:(n_ + 1) * 512]),
                        deps=[t_sil, t_ue, dep(("act", ab, lf))], inc="v")
                    setf(("gs", n2), t_ac)
                    setf(("us", n2), t_ac)
                    setf(("actw", ab, lf), t_ac)
            setf(("wr", sgs), t_gm)
            setf(("wr", sus), t_um)
        wd_slots = []
        for p3 in range(3):
            sl = ring_slot()
            t_wd = sch.emit("gpsimd", lambda e, e_=e_i, p_=p3, s_=sl: e.dma_start(
                out=ft["wring"][:, s_, :].rearrange("p (c f) -> p c f", f=D),
                in_=ap["WdS"][e_, :, p_ * 2:(p_ + 1) * 2, :]),
                deps=[dep(("wr", sl))], inc="gwr%d" % sl, inc_by=16)
            wd_slots.append((sl, t_wd))
        act_deps = [dep(("actw", ab, j)) for j in range(6)]
        for tt in range(8):
            for dq in range(4):
                psl = 4 + (tt * 4 + dq) % 2
                t_dm = None
                for fc in range(6):
                    sl, t_wd = wd_slots[fc // 2]
                    wd_v = ft["wring"][:, sl, :].rearrange("p (c f) -> p c f", f=D)
                    t_dm = sch.emit("tensor", lambda e, f_=fc, tt_=tt, dq_=dq,
                                    sl_=psl, w_=wd_v, a_=ab: e.matmul(
                        ps[:, sl_, :],
                        ft["scr"][:, a_, f_, tt_ * 128:(tt_ + 1) * 128],
                        w_[:, f_ % 2, dq_ * 512:(dq_ + 1) * 512],
                        start=(f_ == 0), stop=(f_ == 5)),
                        deps=([t_wd, dep(("ps", psl))] + act_deps if fc == 0 else [t_wd]),
                        inc="pe" if fc == 5 else None)
                t_ye = sch.emit("vector", lambda e, tt_=tt, dq_=dq, sl_=psl: e.tensor_add(
                    ft["yacc"][:, tt_, dq_ * 512:(dq_ + 1) * 512],
                    ft["yacc"][:, tt_, dq_ * 512:(dq_ + 1) * 512],
                    ps[:, sl_, :]), deps=[t_dm, dep(("yaccw", tt, dq))], inc="v")
                setf(("ps", psl), t_ye)
                setf(("yaccw", tt, dq), t_ye)
                if e_i == N_EXP - 1:
                    setf(("yacc", tt), t_ye)
        pe_now = Tok("pe", sch.cnt["pe"])
        for sl, _ in wd_slots:
            setf(("wr", sl), pe_now)
        for j in range(6):
            setf(("act", ab, j), pe_now)


def build_program(n_seq=SEQ_PER_CORE, do_attn=True, do_ffn=True, taps=False,
                  n_qg=4, n_ft=24, n_tt=NTT):
    nc = bass.Bass("TRN2", target_bir_lowering=False)
    ap = {}
    def din(name, shape, dt=F32):
        ap[name] = nc.dram_tensor(name, shape, dt, kind="ExternalInput").ap()
    din("x", [SEQ_PER_CORE, C, D])
    din("WqS", [8, 128, NC, FT_W]); din("WkvS", [16, 128, NC, FT_W])
    din("WoS", [NH, 4, 128, 512])
    din("gwS", [128, NC, N_EXP])
    din("WgS", [N_EXP, 128, NC, F_EXP]); din("WuS", [N_EXP, 128, NC, F_EXP])
    din("WdS", [N_EXP, 128, 6, D])
    din("SgS", [2, 128, NC, 768]); din("SuS", [2, 128, NC, 768])
    din("SdS", [2, 128, 12, 1024])
    din("gq", [128, NC]); din("bqpre", [128, NC]); din("gkv", [128, NC]); din("bkvpre", [128, NC])
    din("bq_row", [1, D]); din("bkv_row", [1, 2 * D]); din("bo_row", [1, D])
    din("lnh_g", [128, 1]); din("lnh_b", [128, 1])
    din("c_eye", [128, 128], BF16); din("c_eye_f", [128, 128], F32)
    din("c_ones_row", [1, 128], BF16); din("c_ones_col", [128, 1], BF16)
    din("c_cos", [128, NTT, HD2]); din("c_sin", [128, NTT, HD2])

    out_ap = nc.dram_tensor("out", [SEQ_PER_CORE, C, D], F32, kind="ExternalOutput").ap()
    scr_kind = "ExternalOutput" if taps else "Internal"
    qn_scr = nc.dram_tensor("qn_scr", [NH, 128, C], BF16, kind=scr_kind).ap()
    kn_scr = nc.dram_tensor("kn_scr", [NH, 128, C], BF16, kind=scr_kind).ap()
    v_scr = nc.dram_tensor("v_scr", [NH, 128, NTT, 128], BF16, kind=scr_kind).ap()
    xmid_scr = nc.dram_tensor("xmid_scr", [SEQ_PER_CORE * C, D], F32, kind=scr_kind).ap()
    comb_scr = nc.dram_tensor("comb_scr", [N_EXP, CHUNK], BF16, kind="Internal").ap()

    with ExitStack() as es:
        sch = Sched(nc, es)
        sb = {}
        def sbt(name, shape, dt=F32):
            sb[name] = es.enter_context(nc.sbuf_tensor("sb_" + name, shape, dt))
            return sb[name]
        sbt("eye", [128, 128], BF16); sbt("eye_f", [128, 128], F32)
        sbt("ones_row", [1, 128], BF16); sbt("ones_col", [128, 1], BF16)
        sbt("gq", [128, NC]); sbt("bqpre", [128, NC]); sbt("gkv", [128, NC]); sbt("bkvpre", [128, NC])
        sbt("bqpre_bf", [128, NC], BF16); sbt("bkvpre_bf", [128, NC], BF16)
        sbt("bo_row_bf", [1, D], BF16)
        sbt("lnh", [128, 8])
        sbt("cbias", [128, 2])
        sbt("vec", [128, 64])
        sbt("mrot", [128, NTT])
        sbt("dummy_bf", [128, 2048], BF16)

        with nc.Block() as block:
            ld = []
            def pload(dst, src, cast=False):
                eng = "gpsimd" if cast else "sync"
                ld.append(sch.emit(eng, lambda e, d=dst, s=src: e.dma_start(out=d, in_=s),
                                   inc="gm0" if cast else "sper", inc_by=16))
            pload(sb["eye"][:], ap["c_eye"]); pload(sb["eye_f"][:], ap["c_eye_f"])
            pload(sb["ones_row"][:], ap["c_ones_row"]); pload(sb["ones_col"][:], ap["c_ones_col"])
            for n in ("gq", "bqpre", "gkv", "bkvpre"):
                pload(sb[n][:], ap[n])
            pload(sb["bo_row_bf"][:], ap["bo_row"], cast=True)
            pload(sb["lnh"][:, 0:1], ap["lnh_g"]); pload(sb["lnh"][:, 1:2], ap["lnh_b"])
            allld = list(ld)
            t1 = sch.emit("vector", lambda e: e.tensor_copy(sb["bqpre_bf"][:], sb["bqpre"][:]),
                          deps=allld, inc="v")
            t2 = sch.emit("vector", lambda e: e.tensor_copy(sb["bkvpre_bf"][:], sb["bkvpre"][:]),
                          inc="v")
            t4 = sch.emit("vector", lambda e: e.tensor_scalar(
                out=sb["lnh"][:, 2:3], in0=sb["lnh"][:, 0:1], scalar1=S128, scalar2=None,
                op0=ALU.mult), inc="v")
            t5 = sch.emit("vector", lambda e: e.tensor_scalar(
                out=sb["lnh"][:, 3:4], in0=sb["lnh"][:, 1:2], scalar1=S128, scalar2=None,
                op0=ALU.mult), inc="v")
            t6 = sch.emit("vector", lambda e: e.memset(sb["cbias"][:, 0:1], EPS), inc="v")
            t7 = sch.emit("vector", lambda e: e.memset(sb["cbias"][:, 1:2], math.log(MROT)),
                          inc="v")
            persist_tok = [t1, t2, t4, t5, t6, t7] + allld

            xmid_tok = None
            if do_attn:
                with ExitStack() as aes:
                    xmid_tok = emit_attention(nc, sch, sb, ap, aes, persist_tok,
                                              qn_scr, kn_scr, v_scr, xmid_scr, n_seq,
                                              n_qg=n_qg, n_ft=n_ft, n_tt=n_tt)
                    sch.replay(block)
            if do_ffn:
                with ExitStack() as fes:
                    emit_ffn(nc, sch, sb, ap, fes, persist_tok, xmid_tok, xmid_scr,
                             comb_scr, out_ap, n_chunk=n_seq * 2)
                    sch.replay(block)
            sch.wait_only("sync", sch.final_toks())
            sch.replay(block)
    return nc


def kernel(**inputs):
    sh, xs = prep_host(inputs)
    nc = build_program()
    in_maps = []
    for c in range(N_CORES):
        m = dict(sh)
        m["x"] = xs[c]
        in_maps.append(m)
    res = run_bass_kernel_spmd(nc, in_maps, core_ids=list(range(N_CORES)))
    outs = [np.asarray(res.results[c]["out"]).reshape(SEQ_PER_CORE, C, D)
            for c in range(N_CORES)]
    return np.concatenate(outs, axis=0)
